# version 14
# baseline (speedup 1.0000x reference)
import numpy as np
import ml_dtypes
import contextlib
import concourse.bass as bass
import concourse.mybir as mybir
from concourse.bass_utils import run_bass_kernel_spmd

F32 = mybir.dt.float32
F16 = mybir.dt.float16
BF16 = mybir.dt.bfloat16
FP8 = mybir.dt.float8e4
AF = mybir.ActivationFunctionType
ALU = mybir.AluOpType
AX = mybir.AxisListType

NCORES = 8


class Cfg:
    def __init__(self, B=2, T=8192, H=2048, I=8192):
        self.B, self.T, self.H, self.I = B, T, H, I
        self.COLS = 2 * H
        self.NCHUNK = NCORES // B
        self.CT = T // self.NCHUNK
        assert self.CT % 512 == 0 and H % 128 == 0 and I % 512 == 0
        self.NTT = self.CT // 128
        self.NH = H // 128
        self.NI = I // 128
        self.ICH = 512
        self.NICH = I // self.ICH
        self.SEC = 1024 if I % 1024 == 0 else I
        self.NSEC = I // self.SEC
        self.NIS = self.SEC // 128
        self.CTP = self.CT + 128
        self.NT512 = self.CT // 512
        self.NH512 = H // 512


FULL = Cfg()



class Prog:
    ENGS = ("pe", "dv", "ac", "gp", "sp")

    def __init__(self):
        self.ops = []
        self.n = {e: 0 for e in self.ENGS}

    def emit(self, eng, fn, deps=(), asyn=False, amt=1, tag=""):
        tok = (eng, self.n[eng])
        self.ops.append(dict(e=eng, i=self.n[eng], fn=fn,
                             deps=[d for d in deps if d is not None],
                             asyn=asyn, amt=amt, marked=False, tag=tag))
        self.n[eng] += 1
        return tok

    def dma(self, eng, out, in_, deps=(), transpose=False, tag=""):
        if transpose:
            eng = "sp"
            fn = (lambda E, o=out, i=in_: E.dma_start(out=o, in_=i, transpose=True))
        else:
            fn = (lambda E, o=out, i=in_: E.dma_start(out=o, in_=i))
        return self.emit(eng, fn, deps, asyn=True, amt=16, tag=tag)

    NPOOL = 8

    def finalize(self, nc, blk, stack):
        byeng = {e: [] for e in self.ENGS}
        for op in self.ops:
            byeng[op["e"]].append(op)
        idx = {(op["e"], op["i"]): op for op in self.ops}
        for op in self.ops:
            for d in op["deps"]:
                pr = idx[d]
                if pr["asyn"]:
                    continue
                if pr["e"] == "pe" and op["e"] == "pe":
                    continue
                pr["marked"] = True
        sems = {}
        pool = {}
        for e in self.ENGS:
            sems[e] = stack.enter_context(nc.semaphore(f"s_{e}"))
        for e in self.ENGS:
            aops = [op for op in byeng[e] if op["asyn"] and op["amt"] == 16]
            ccops = [op for op in byeng[e] if op["asyn"] and op["amt"] != 16]
            for j, op in enumerate(ccops):
                lane = self.NPOOL + j
                key = (e, lane)
                pool[key] = stack.enter_context(nc.semaphore(f"q_{e}cc{j}"))
                op["lane"] = lane
                op["chain_wait"] = 0
                op["val"] = op["amt"]
                op["marked"] = True
            lane_cnt = {}
            for j, op in enumerate(aops):
                lane = j % self.NPOOL
                key = (e, lane)
                if key not in pool:
                    pool[key] = stack.enter_context(nc.semaphore(f"q_{e}{lane}"))
                prev = lane_cnt.get(lane, 0)
                op["lane"] = lane
                op["chain_wait"] = prev
                op["val"] = prev + op["amt"]
                lane_cnt[lane] = op["val"]
                op["marked"] = True
            v = 0
            for op in byeng[e]:
                if op["asyn"]:
                    continue
                if op["marked"]:
                    v += op["amt"]
                op["val"] = v

        def mk(e):
            def run(E):
                last = {}
                for op in byeng[e]:
                    wants = {}
                    for d in op["deps"]:
                        pr = idx[d]
                        if pr["asyn"]:
                            k = (pr["e"], pr["lane"])
                            wants[k] = max(wants.get(k, 0), pr["val"])
                        elif pr["e"] == "pe" and e == "pe":
                            continue
                        else:
                            wants[pr["e"]] = max(wants.get(pr["e"], 0), pr["val"])
                    if op.get("asyn") and op["chain_wait"] > 0:
                        k = (e, op["lane"])
                        wants[k] = max(wants.get(k, 0), op["chain_wait"])
                    for k, v in wants.items():
                        if last.get(k, 0) < v:
                            E.wait_ge(pool[k] if isinstance(k, tuple) else sems[k], v)
                            last[k] = v
                    ins = op["fn"](E)
                    if op["asyn"]:
                        ins.then_inc(pool[(e, op["lane"])], op["amt"])
                    elif op["marked"]:
                        ins.then_inc(sems[e], op["amt"])
            return run

        blk.tensor(mk("pe"))
        blk.vector(mk("dv"))
        blk.scalar(mk("ac"))
        blk.gpsimd(mk("gp"))
        blk.sync(mk("sp"))


class Arena:

    def __init__(self, nc, stack, name, nelem, dt, **kw):
        self.t = stack.enter_context(nc.sbuf_tensor(name, [128, nelem], dt, **kw))
        self.nelem = nelem
        self.off = 0
        self.marks = []

    def alloc(self, shape):
        n = 1
        for s in shape:
            n *= s
        o = (self.off + 63) // 64 * 64
        self.off = o + n
        assert self.off <= self.nelem, f"arena overflow {self.off} > {self.nelem}"
        ap = self.t.ap()[:, o:o + n]
        if len(shape) > 1:
            kw = {}
            names = "abcd"[: len(shape)]
            pat = f"p ({' '.join(names)}) -> p {' '.join(names)}"
            kw = {names[j]: shape[j] for j in range(1, len(shape))}
            ap = ap.rearrange(pat, **kw)
        return ap

    def mark(self):
        self.marks.append(self.off)

    def reset(self):
        self.off = self.marks.pop()


CONSTS = {"one": 1.0, "half": 0.5}
CIDX = {k: i for i, k in enumerate(CONSTS)}


def build_program(cfg: Cfg, stop_after=None, debug_dumps=False):
    nc = bass.Bass("TRN2", target_bir_lowering=False)
    c = cfg
    H, I, CT = c.H, c.I, c.CT
    stack = contextlib.ExitStack()
    p = Prog()

    def din(name, shape, dt=F32):
        return nc.dram_tensor(name, shape, dt, kind="ExternalInput").ap()

    x_loc = din("x_loc", [CT, H])
    x_pre = din("x_pre", [128, H])
    wxyT = din("wxyT", [H, c.COLS])
    woutT = din("woutT", [H, H])
    wi_in = din("wi_in", [128, c.NH, 128])
    wa_in = din("wa_in", [128, c.NH, 128])
    bi_in = din("bi_in", [128, c.NH])
    ba_in = din("ba_in", [128, c.NH])
    ap_in = din("ap_in", [128, c.NH])
    cw_in = din("cw_in", [128, c.NH, 4])
    cb_in = din("cb_in", [128, c.NH])
    ISH = I // NCORES
    DSH = I // NCORES
    gw_in = din("gw_sh", [ISH, H])
    uw_in = din("uw_sh", [ISH, H])
    dw_in = din("dw_sh", [H, DSH])
    ident_in = din("ident_in", [128, 128], BF16)
    consts_in = din("consts_in", [128, len(CONSTS)])
    cmask_in = din("cmask_in", [128, c.NH, 6])
    out_d = nc.dram_tensor("out", [CT, H], F32, kind="ExternalOutput").ap()

    def dscr(name, shape, dt, **kw):
        return nc.dram_tensor(name, shape, dt, kind="Internal", **kw).ap()

    xnbf_s = dscr("xnbf_s", [c.CTP, H], BF16)
    xnq_s = dscr("xnq_s", [CT, H], BF16)
    WSH = ISH * H + ISH * H + H * DSH
    OFF_G, OFF_U, OFF_D = 0, ISH * H, 2 * ISH * H
    anx_s = dscr("anx_s", [2, c.NH, 128, CT], BF16)
    hq_s = dscr("hq_s", [CT, I], BF16)
    att_s = dscr("att_s", [CT, H], F32)
    from concourse.replica_groups import maybe_share_collective_output_space
    RG = [list(range(NCORES))]
    ccsp = maybe_share_collective_output_space("AllGather", RG, prefer_local=True)
    cc_in = dscr("cc_in", [2, 128, c.NH], F32)
    cc_out = dscr("cc_out", [NCORES * 2, 128, c.NH], F32, addr_space=ccsp)
    wsh_in = dscr("wsh_in", [1, WSH], BF16)
    wsh_out = dscr("wsh_out", [NCORES, WSH], BF16, addr_space=ccsp)

    def gwT_view(rk, off):
        return wsh_out[rk:rk + 1, off:off + ISH * H].rearrange(
            "q (a b) -> (q a) b", b=ISH)

    def dwT_view(rk):
        return wsh_out[rk:rk + 1, OFF_D:OFF_D + H * DSH].rearrange(
            "q (a b) -> (q a) b", b=H)

    abf = Arena(nc, stack, "abf", 78 * 1024, BF16, is_transpose_dest=True)
    af32 = Arena(nc, stack, "af32", 10 * 1024, F32)
    af8 = Arena(nc, stack, "af8", 5 * 1024, FP8)

    csts = af32.alloc([len(CONSTS)])
    bi_sb = af32.alloc([c.NH])
    ba_sb = af32.alloc([c.NH])
    msp_sb = af32.alloc([c.NH])
    cw_sb = af32.alloc([c.NH, 4])
    cb_sb = af32.alloc([c.NH])
    cmask_sb = af32.alloc([c.NH, 6])
    tiny = af32.alloc([2, max(16, c.NH)])
    hfin = af32.alloc([c.NH])
    Ach = af32.alloc([c.NH])
    ssum = af32.alloc([c.NH])
    h_in = af32.alloc([c.NH])
    gath = af32.alloc([16, c.NH])
    cand = af32.alloc([6, c.NH])
    wi_sb = abf.alloc([c.NH, 128])
    wa_sb = abf.alloc([c.NH, 128])

    CST = lambda name: csts[:, CIDX[name]:CIDX[name] + 1]

    ld_csts = p.dma("sp", csts, consts_in, tag="csts")
    ld_wi = p.dma("gp", wi_sb, wi_in, tag="wi")
    ld_wa = p.dma("gp", wa_sb, wa_in, tag="wa")
    ld_bi = p.dma("sp", bi_sb, bi_in)
    ld_ba = p.dma("sp", ba_sb, ba_in)
    ld_ap = p.dma("sp", msp_sb, ap_in)
    ld_cw = p.dma("sp", cw_sb, cw_in)
    ld_cb = p.dma("sp", cb_sb, cb_in)
    ld_cm = p.dma("sp", cmask_sb, cmask_in)

    t_e = p.emit("ac", lambda E: E.activation(tiny[:, 0, 0:c.NH], msp_sb, AF.Exp),
                 deps=[ld_ap])
    t_l = p.emit("ac", lambda E: E.activation(tiny[:, 1, 0:c.NH], tiny[:, 0, 0:c.NH],
                                              AF.Ln, bias=CST("one")),
                 deps=[t_e, ld_csts])
    t_m = p.emit("dv", lambda E: E.tensor_scalar_mul(msp_sb, tiny[:, 1, 0:c.NH], -8.0),
                 deps=[t_l])

    psA = stack.enter_context(nc.psum_tensor("psA", [128, 2048], F32))
    psB = stack.enter_context(nc.psum_tensor("psB", [128, 2048], F32))

    def quant_tile(x_ap, q8_ap, ma, s2, rs, outbf_ap, deps, dq_eng="gp"):
        feat = x_ap.shape[-1]
        nb = feat // 32
        x3 = x_ap.rearrange("p (a b) -> p a b", b=32)
        t_ma = p.emit("dv", lambda E: E.tensor_reduce(ma[:, 0:nb], x3, AX.X, ALU.max,
                                                      apply_absolute_value=True), deps=deps)
        t_s2 = p.emit("dv", lambda E: E.tensor_scalar(s2[:, 0:nb], ma[:, 0:nb],
                                                      1.0 / 224.0, 2e-30, ALU.mult,
                                                      ALU.max), deps=[t_ma])
        t_rs = p.emit("dv", lambda E: E.reciprocal(rs[:, 0:nb], s2[:, 0:nb]), deps=[t_s2])
        q83 = q8_ap.rearrange("p (a b) -> p a b", b=32)
        t_q8 = p.emit("dv", lambda E: E.tensor_tensor(
            q83, x3, rs[:, 0:nb].unsqueeze(2).broadcast_to([128, nb, 32]), ALU.mult),
            deps=[t_rs])
        o3 = outbf_ap.rearrange("p (a b) -> p a b", b=32)
        t_dq = p.emit(dq_eng, lambda E: E.tensor_tensor(
            o3, q83, s2[:, 0:nb].unsqueeze(2).broadcast_to([128, nb, 32]), ALU.mult),
            deps=[t_q8])
        return t_dq, t_q8

    af32.mark(); abf.mark(); af8.mark()
    WF = min(2048, max(H, 2048))
    NWB = 2
    wq_x = af32.alloc([NWB, WF])
    wq_bf = abf.alloc([NWB, WF])
    wq_q8 = af8.alloc([1, WF])
    wq_ma = af32.alloc([NWB, WF // 32])
    wq_s2 = af32.alloc([NWB, WF // 32])
    wq_rs = af32.alloc([NWB, WF // 32])

    wqT_bf = abf.alloc([NWB, WF // 128, 128])
    ident = abf.alloc([128])
    ld_id = p.dma("gp", ident, ident_in, tag="ident")

    _wst = dict(terms={}, trp={}, q8prev={}, n=0)

    def emit_w_tile(srcw, dstT, r, feat, deps0=()):
        i = _wst["n"]; _wst["n"] += 1
        bi = i % NWB
        qi = 0
        ld = p.dma("ac", wq_x[:, bi, 0:feat], srcw[128 * r:128 * (r + 1), :],
                   deps=list(deps0) + [_wst["terms"].get(bi)], tag="wq_ld")
        dq, q8t = quant_tile(wq_x[:, bi, 0:feat], wq_q8[:, qi, 0:feat],
                             wq_ma[:, bi, :], wq_s2[:, bi, :], wq_rs[:, bi, :],
                             wq_bf[:, bi, 0:feat],
                             [ld, _wst["q8prev"].get(qi)], dq_eng="gp")
        _wst["q8prev"][qi] = dq
        evs = []
        for j in range(feat // 128):
            sl = (j % 4) * 512
            tr = p.emit("pe", (lambda E, bi=bi, j=j, sl=sl: E.transpose(
                psA[:, sl:sl + 64].bitcast(BF16),
                wq_bf[:, bi, 128 * j:128 * (j + 1)], ident)),
                deps=[dq, ld_id, _wst["trp"].get(j % 4)], tag="wtr")
            ev = p.emit("dv", (lambda E, bi=bi, j=j, sl=sl: E.tensor_copy(
                wqT_bf[:, bi, j, :], psA[:, sl:sl + 64].bitcast(BF16))),
                deps=[tr, _wst["terms"].get(bi)], tag="wtr_ev")
            _wst["trp"][j % 4] = ev
            evs.append(ev)
        st = p.dma("ac",
                   dstT[:, 128 * r:128 * (r + 1)].rearrange(
                       "(j q) c -> q j c", q=128),
                   wqT_bf[:, bi, 0:feat // 128, :], deps=evs, tag="wq_st")
        _wst["terms"][bi] = st
        return st

    gq_view = wsh_in[0:1, OFF_G:OFF_G + ISH * H].rearrange("q (a b) -> (q a) b", b=ISH)
    uq_view = wsh_in[0:1, OFF_U:OFF_U + ISH * H].rearrange("q (a b) -> (q a) b", b=ISH)
    dq_view = wsh_in[0:1, OFF_D:OFF_D + H * DSH].rearrange("q (a b) -> (q a) b", b=H)
    w_tiles = ([(gw_in, gq_view, r, H) for r in range(ISH // 128)]
               + [(uw_in, uq_view, r, H) for r in range(ISH // 128)]
               + [(dw_in, dq_view, r, DSH) for r in range(H // 128)])
    NAB = 2
    ax = abf.alloc([NAB, 2 * H]).bitcast(F32)
    axn = abf.alloc([NAB, 2 * H]).bitcast(F32)
    axnb = abf.alloc([NAB, H])
    axqb = abf.alloc([NAB, H])
    ast = af32.alloc([NAB, 8])
    ama = af32.alloc([NAB, H // 32])
    as2 = af32.alloc([NAB, H // 32])
    ars = af32.alloc([NAB, H // 32])
    axq8 = af8.alloc([1, H])
    _ast = dict(q8prev={}, n=0)

    def phaseA_tile(src_rows, dst_bf, dst_q, bi, deps_in):
        ld = p.dma("ac", ax[:, bi, :], src_rows, deps=deps_in, tag="A_ld")
        s1 = p.emit("dv", lambda E: E.reduce_sum(ast[:, bi, 0:1], ax[:, bi, :], AX.X),
                    deps=[ld])
        ssq = p.emit("ac", lambda E: E.activation(axn[:, bi, :], ax[:, bi, :],
                                                  AF.Square,
                                                  accum_out=ast[:, bi, 2:3]),
                     deps=[ld], tag="ssq")
        m = p.emit("dv", lambda E: E.tensor_scalar_mul(ast[:, bi, 1:2], ast[:, bi, 0:1],
                                                       1.0 / H), deps=[s1])
        m2 = p.emit("dv", lambda E: E.tensor_tensor(ast[:, bi, 3:4], ast[:, bi, 1:2],
                                                    ast[:, bi, 1:2], ALU.mult), deps=[m])
        m2e = p.emit("dv", lambda E: E.tensor_scalar(ast[:, bi, 4:5], ast[:, bi, 3:4],
                                                     1.0, 1e-5, ALU.mult, ALU.subtract),
                     deps=[m2])
        var = p.emit("dv", lambda E: E.tensor_scalar(ast[:, bi, 5:6], ast[:, bi, 2:3],
                                                     1.0 / H, ast[:, bi, 4:5], ALU.mult,
                                                     ALU.subtract), deps=[ssq, m2e])
        r1 = p.emit("dv", lambda E: E.reciprocal(ast[:, bi, 6:7], ast[:, bi, 5:6]),
                    deps=[var])
        rstd = p.emit("ac", lambda E: E.activation(ast[:, bi, 7:8], ast[:, bi, 6:7],
                                                   AF.Sqrt), deps=[r1])
        xn = p.emit("dv", lambda E: E.tensor_scalar(axn[:, bi, :], ax[:, bi, :],
                                                    ast[:, bi, 1:2], ast[:, bi, 7:8],
                                                    ALU.subtract, ALU.mult),
                    deps=[rstd, ssq], tag="xn")
        xnb = p.emit("ac", lambda E: E.activation(axnb[:, bi, :], axn[:, bi, :],
                                                  AF.Copy), deps=[xn], tag="xnb")
        stb = p.dma("gp", dst_bf, axnb[:, bi, :], deps=[xnb], tag="A_stb")
        done = [stb]
        if dst_q is not None:
            qi = 0
            _ast["n"] += 1
            dq, _ = quant_tile(axn[:, bi, :], axq8[:, qi, :], ama[:, bi, :],
                               as2[:, bi, :], ars[:, bi, :],
                               axqb[:, bi, :],
                               [xn, _ast["q8prev"].get(qi)], dq_eng="gp")
            _ast["q8prev"][qi] = dq
            stq = p.dma("gp", dst_q, axqb[:, bi, :], deps=[dq], tag="A_stq")
            done.append(stq)
        return done

    a_done_tiles = []
    wq_sts = []
    a_idx = 0
    nW, nA = len(w_tiles), c.NTT + 1

    def emit_a(i):
        bi = i % NAB
        deps_in = a_done_tiles[i - NAB] if i >= NAB else []
        if i < c.NTT:
            a_done_tiles.append(phaseA_tile(
                x_loc[128 * i:128 * (i + 1), :],
                xnbf_s[128 * i:128 * (i + 1), :],
                xnq_s[128 * i:128 * (i + 1), :], bi, deps_in))
        else:
            a_done_tiles.append(phaseA_tile(x_pre, xnbf_s[CT:CT + 128, :], None,
                                            bi, deps_in))

    for wi_, (srcw, dstT, r, feat) in enumerate(w_tiles):
        wq_sts.append(emit_w_tile(srcw, dstT, r, feat,
                                  deps0=[t_m] if wi_ == 0 else []))
        while a_idx < nA and a_idx * nW <= (wi_ + 1) * nA:
            emit_a(a_idx)
            a_idx += 1
    while a_idx < nA:
        emit_a(a_idx)
        a_idx += 1
    a_done = [t for tile in a_done_tiles for t in tile]

    ccw = p.emit("gp", lambda E: E.collective_compute(
        "AllGather", ALU.bypass, RG, [wsh_in.flatten()], [wsh_out.flatten()]),
        deps=wq_sts, asyn=True, amt=1, tag="allgather_w")
    wq_done = [ccw]
    af32.reset(); abf.reset(); af8.reset()
    if stop_after == "A":
        return _stop(p, nc, stack, a_done)

    abf.mark()
    xnT = abf.alloc([c.NH, CT])
    wxy_pan = abf.alloc([2, c.NH, 128])
    xnT_toks = []
    for h in range(c.NH):
        xnT_toks.append(p.dma("sp", xnT[:, h, :],
                              xnbf_s[0:CT, 128 * h:128 * (h + 1)],
                              deps=a_done + wq_sts, transpose=True, tag="xnT"))

    abf.mark(); af32.mark()
    xnTp = abf.alloc([c.NH, 128])
    for h in range(c.NH):
        xnT_toks.append(p.dma("sp", xnTp[:, h, :],
                              xnbf_s[CT:c.CTP, 128 * h:128 * (h + 1)],
                              deps=a_done, transpose=True, tag="xnTp"))
    xp = abf.alloc([CT + 4])
    cacc = abf.alloc([2, CT])
    xr2 = abf.alloc([2, CT])
    gx2 = abf.alloc([2, CT])
    ga2 = abf.alloc([2, CT])
    ab2 = abf.alloc([2, CT])
    nx2 = abf.alloc([2, CT])
    loga = af32.alloc([2, CT])
    sscr = af32.alloc([CT])
    mscr = af32.alloc([CT])

    def load_wxy_panel(colt, bi, deps):
        src = wxyT[:, 128 * colt:128 * (colt + 1)].rearrange("(a q) e -> q a e", q=128)
        return p.dma("gp", wxy_pan[:, bi, :, :], src, deps=deps, tag="wxy")

    def b1_stage1(s, pan_tok, gi, prev):
        bi = s % 2
        mm_last = None
        for h in range(c.NH):
            d = []
            if h == 0:
                d = [pan_tok] + xnT_toks + [prev.get("psA"), prev.get("slot1")]
            for j in range(c.NT512):
                mm_last = p.emit("pe", (lambda E, h=h, j=j, bi=bi: E.matmul(
                    psA[:, 512 * j:512 * (j + 1)], wxy_pan[:, bi, h, :],
                    xnT[:, h, 512 * j:512 * (j + 1)],
                    start=(h == 0), stop=(h == c.NH - 1))),
                    deps=d if j == 0 else [])
                d = []
            mm_last = p.emit("pe", (lambda E, h=h, bi=bi: E.matmul(
                psB[:, 0:3], wxy_pan[:, bi, h, :], xnTp[:, h, 0:3],
                start=(h == 0), stop=(h == c.NH - 1))),
                deps=[prev.get("psB")] if h == 0 else [])
        evp = p.emit("dv", lambda E: E.tensor_copy(xp[:, 0:3], psB[:, 0:3]),
                     deps=[mm_last, prev.get("xp")])
        evx = p.emit("dv", lambda E: E.tensor_copy(xp[:, 3:3 + CT], psA[:, 0:CT]),
                     deps=[mm_last])
        c0 = p.emit("dv", lambda E, s=s: E.tensor_scalar(
            cacc[:, 0, :], xp[:, 0:CT], cw_sb[:, s, 0:1], cb_sb[:, s:s + 1],
            ALU.mult, ALU.add), deps=[evp, evx, ld_cw, ld_cb, prev.get("cacc")])
        c1 = p.emit("dv", lambda E, s=s: E.scalar_tensor_tensor(
            cacc[:, 1, :], xp[:, 1:1 + CT], cw_sb[:, s, 1:2], cacc[:, 0, :],
            ALU.mult, ALU.add), deps=[c0])
        c2 = p.emit("dv", lambda E, s=s: E.scalar_tensor_tensor(
            cacc[:, 0, :], xp[:, 2:2 + CT], cw_sb[:, s, 2:3], cacc[:, 1, :],
            ALU.mult, ALU.add), deps=[c1])
        c3 = p.emit("dv", lambda E, s=s, gi=gi: E.scalar_tensor_tensor(
            xr2[:, gi, :], xp[:, 3:3 + CT], cw_sb[:, s, 3:4], cacc[:, 0, :],
            ALU.mult, ALU.add), deps=[c2, prev.get("slot2")])
        gmm = []
        for j in range(c.NT512):
            m1 = p.emit("pe", (lambda E, j=j, s=s, gi=gi: E.matmul(
                psA[:, 512 * j:512 * (j + 1)], wi_sb[:, s, :],
                xr2[:, gi, 512 * j:512 * (j + 1)], start=True, stop=True)),
                deps=[c3, evx, ld_wi] if j == 0 else [])
            m2 = p.emit("pe", (lambda E, j=j, s=s, gi=gi: E.matmul(
                psB[:, 512 * j:512 * (j + 1)], wa_sb[:, s, :],
                xr2[:, gi, 512 * j:512 * (j + 1)], start=True, stop=True)),
                deps=[evp, ld_wa] if j == 0 else [])
            gmm += [m1, m2]
        sx = p.emit("ac", lambda E, s=s, gi=gi: E.activation(
            gx2[:, gi, :], psA[:, 0:CT], AF.Sigmoid, bias=bi_sb[:, s:s + 1]),
            deps=gmm + [ld_bi, prev.get("slot3")], tag="sigmoid")
        sa = p.emit("ac", lambda E, s=s, gi=gi: E.activation(
            ga2[:, gi, :], psB[:, 0:CT], AF.Sigmoid, bias=ba_sb[:, s:s + 1]),
            deps=gmm + [ld_ba, prev.get("slot4")], tag="sigmoid")
        return dict(sx=sx, sa=sa, xr=c3, xp_last=c3, mm_last=mm_last,
                    cacc_last=c3, evp=evp)

    def b1_stage2(s, gi, st1, prev2):
        la = p.emit("dv", lambda E, gi=gi, s=s: E.tensor_scalar_mul(
            loga[:, gi, :], ga2[:, gi, :], msp_sb[:, s:s + 1]), deps=[st1["sa"], t_m])
        a_b = p.emit("ac", lambda E, gi=gi: E.activation(ab2[:, gi, :], loga[:, gi, :],
                                                         AF.Exp),
                     deps=[la, prev2.get("a_st"), prev2.get("scan")], tag="lnexp")
        aq = p.emit("dv", lambda E, gi=gi: E.tensor_tensor(sscr[:], ab2[:, gi, :],
                                                           ab2[:, gi, :], ALU.mult),
                    deps=[a_b, prev2.get("nx")])
        m1 = p.emit("dv", lambda E: E.tensor_scalar(mscr[:], sscr[:], -1.0, 1.0,
                                                    ALU.mult, ALU.add),
                    deps=[aq, prev2.get("hf")])
        ln = p.emit("ac", lambda E: E.activation(sscr[:], mscr[:], AF.Ln),
                    deps=[m1], tag="lnexp")
        ex = p.emit("ac", lambda E: E.activation(mscr[:], sscr[:], AF.Exp,
                                                 scale=CST("half")),
                    deps=[ln, ld_csts], tag="lnexp")
        n1 = p.emit("dv", lambda E, gi=gi: E.tensor_tensor(sscr[:], xr2[:, gi, :],
                                                           gx2[:, gi, :], ALU.mult),
                    deps=[st1["xr"], st1["sx"], ex])
        nx = p.emit("dv", lambda E, gi=gi: E.tensor_tensor(nx2[:, gi, :], sscr[:],
                                                           mscr[:], ALU.mult),
                    deps=[n1, prev2.get("n_st"), prev2.get("scan")])
        asum = p.emit("dv", lambda E, gi=gi, s=s: E.reduce_sum(ssum[:, s:s + 1],
                                                               loga[:, gi, :], AX.X),
                      deps=[la])
        aex = p.emit("ac", lambda E, s=s: E.activation(Ach[:, s:s + 1], ssum[:, s:s + 1],
                                                       AF.Exp), deps=[asum], tag="lnexp")
        sc = p.emit("dv", lambda E, gi=gi: E.tensor_tensor_scan(
            mscr[:], ab2[:, gi, :], nx2[:, gi, :], 0.0, ALU.mult, ALU.add),
            deps=[nx], tag="scan1")
        hf = p.emit("dv", lambda E, s=s: E.tensor_copy(hfin[:, s:s + 1],
                                                       mscr[:, CT - 1:CT]), deps=[sc])
        sta = p.dma("gp", anx_s[0, s], ab2[:, gi, :], deps=[a_b], tag="anx_a")
        stn = p.dma("gp", anx_s[1, s], nx2[:, gi, :], deps=[nx], tag="anx_n")
        return dict(a_st=sta, n_st=stn, hf=hf, aex=aex, la=la, nx=nx, scan=sc)

    GROUP = 2
    pan_toks = {}
    pan_free = {0: None, 1: None}
    for s0 in range(min(2, c.NH)):
        pan_toks[s0] = load_wxy_panel(s0, s0 % 2, [])
    st1_prev = {"sx": wq_sts[-1], "sa": wq_sts[-1]}
    st2_prev = {0: {}, 1: {}}
    strip2 = {}
    s_list = list(range(c.NH))
    for g0 in range(0, c.NH, GROUP):
        grp = s_list[g0:g0 + GROUP]
        st1s = {}
        for gi, s in enumerate(grp):
            prev = dict(
                psA=st1_prev.get("sx"), psB=st1_prev.get("sa"),
                xp=st1_prev.get("xp_last"), cacc=st1_prev.get("cacc_last"),
                slot1=st2_prev[gi].get("la"),
                slot2=st2_prev[gi].get("nx"),
                slot3=st2_prev[gi].get("nx"),
                slot4=st2_prev[gi].get("la"),
            )
            st1s[gi] = b1_stage1(s, pan_toks[s], gi, prev)
            pan_free[s % 2] = st1s[gi]["mm_last"]
            st1_prev = st1s[gi]
            if (s + 2) < c.NH and (s + 2) not in pan_toks:
                pan_toks[s + 2] = load_wxy_panel(s + 2, (s + 2) % 2,
                                                 [pan_free[(s + 2) % 2]])
        for gi, s in enumerate(grp):
            st2 = b1_stage2(s, gi, st1s[gi], st2_prev[gi])
            st2_prev[gi] = st2
            strip2[s] = st2

    if stop_after == "B1":
        return _stop(p, nc, stack,
                     [strip2[s]["hf"] for s in range(c.NH)]
                     + [strip2[s]["a_st"] for s in range(c.NH)]
                     + [strip2[s]["n_st"] for s in range(c.NH)]
                     + [strip2[s]["aex"] for s in range(c.NH)])
    hA_st1 = p.dma("ac", cc_in[0], hfin, deps=[strip2[s]["hf"] for s in range(c.NH)],
                   tag="cc_st")
    hA_st2 = p.dma("ac", cc_in[1], Ach, deps=[strip2[s]["aex"] for s in range(c.NH)],
                   tag="cc_st")
    cc = p.emit("gp", lambda E: E.collective_compute(
        "AllGather", ALU.bypass, RG, [cc_in.flatten()], [cc_out.flatten()]),
        deps=[hA_st1, hA_st2], asyn=True, amt=1, tag="allgather")
    ld_g = p.dma("ac", gath, cc_out.rearrange("r p n -> p r n"), deps=[cc], tag="cc_ld")
    cand_toks = []
    for b in range(c.B):
        rb = b * c.NCHUNK
        t0 = p.emit("dv", lambda E, b=b, rb=rb: E.tensor_copy(
            cand[:, 3 * b + 0, :], gath[:, 2 * rb + 0, :]), deps=[ld_g])
        prevt = t0
        for kk in range(1, 3):
            r = rb + kk
            tm = p.emit("dv", lambda E, b=b, kk=kk, r=r: E.tensor_tensor(
                tiny[:, 0, 0:c.NH], gath[:, 2 * r + 1, :],
                cand[:, 3 * b + kk - 1, :], ALU.mult), deps=[prevt])
            prevt = p.emit("dv", lambda E, b=b, kk=kk, r=r: E.tensor_tensor(
                cand[:, 3 * b + kk, :], tiny[:, 0, 0:c.NH], gath[:, 2 * r + 0, :],
                ALU.add), deps=[tm])
        cand_toks.append(prevt)
    hz = p.emit("dv", lambda E: E.memset(h_in, 0.0), deps=[])
    prevt = hz
    for j in range(6):
        tm = p.emit("dv", lambda E, j=j: E.tensor_tensor(
            tiny[:, 1, 0:c.NH], cand[:, j, :], cmask_sb[:, :, j], ALU.mult),
            deps=cand_toks + [ld_cm, prevt])
        prevt = p.emit("dv", lambda E: E.tensor_tensor(
            h_in, h_in, tiny[:, 1, 0:c.NH], ALU.add), deps=[tm])
    hin_tok = prevt

    if stop_after == "CC":
        return _stop(p, nc, stack, [hin_tok])
    abf.reset(); af32.reset()
    abf.mark(); af32.mark()
    a_rb = af32.alloc([CT]).bitcast(BF16).rearrange("p (a b) -> p a b", a=2)
    n_rb = af32.alloc([CT]).bitcast(BF16).rearrange("p (a b) -> p a b", a=2)
    gyh = abf.alloc([c.NH, CT])
    hstr = af32.alloc([CT])

    b2_guard = ([strip2[s]["a_st"] for s in range(c.NH)]
                + [strip2[s]["n_st"] for s in range(c.NH)]
                + [strip2[c.NH - 1]["nx"], strip2[max(0, c.NH - 2)]["nx"]])
    prev_gy = None
    for s in range(c.NH):
        bi = s % 2
        ypan = load_wxy_panel(c.NH + s, bi, [pan_free[bi]])
        mm_last = None
        for h in range(c.NH):
            d = []
            if h == 0:
                d = [ypan, prev_gy]
                if s == 0:
                    d += [st1_prev["sx"]] + b2_guard
            for j in range(c.NT512):
                mm_last = p.emit("pe", (lambda E, h=h, j=j, bi=bi: E.matmul(
                    psA[:, 512 * j:512 * (j + 1)], wxy_pan[:, bi, h, :],
                    xnT[:, h, 512 * j:512 * (j + 1)],
                    start=(h == 0), stop=(h == c.NH - 1))), deps=d if j == 0 else [])
                d = []
        pan_free[bi] = mm_last
        gy = p.emit("ac", lambda E, s=s: E.activation(gyh[:, s, :], psA[:, 0:CT],
                                                      AF.Gelu),
                    deps=[mm_last, prev_gy], tag="gelu")
        prev_gy = gy
    last_gy = prev_gy
    b2 = {}
    prev_gh = None
    rb_toks = {}
    for s in range(min(2, c.NH)):
        rb_toks[s] = (p.dma("ac", a_rb[:, s % 2, :], anx_s[0, s],
                            deps=b2_guard + [strip2[s]["a_st"]], tag="a_rb"),
                      p.dma("ac", n_rb[:, s % 2, :], anx_s[1, s],
                            deps=[strip2[s]["n_st"]], tag="n_rb"))
    for s in range(c.NH):
        bi = s % 2
        at, ntk = rb_toks[s]
        sc = p.emit("dv", lambda E, s=s, bi=bi: E.tensor_tensor_scan(
            hstr, a_rb[:, bi, :], n_rb[:, bi, :], h_in[:, s:s + 1],
            ALU.mult, ALU.add),
            deps=[at, ntk, hin_tok, prev_gh], tag="scan2")
        gh = p.emit("dv", lambda E, s=s: E.tensor_tensor(gyh[:, s, :], gyh[:, s, :],
                                                         hstr, ALU.mult),
                    deps=[sc, last_gy], tag="gh")
        prev_gh = gh
        b2[s] = dict(gh=gh)
        if s + 2 < c.NH:
            rb_toks[s + 2] = (p.dma("ac", a_rb[:, bi, :], anx_s[0, s + 2],
                                    deps=[sc, strip2[s + 2]["a_st"]], tag="a_rb"),
                              p.dma("ac", n_rb[:, bi, :], anx_s[1, s + 2],
                                    deps=[sc, strip2[s + 2]["n_st"]], tag="n_rb"))

    if stop_after == "B2":
        return _stop(p, nc, stack, [b2[s]["gh"] for s in range(c.NH)])
    abf2_guard = [b2[s]["gh"] for s in range(c.NH)] + [pan_free[0], pan_free[1]]
    woutb = xnT
    af32.reset(); af32.mark()
    attb = af32.alloc([2, H])
    wout_toks = []
    for r in range(c.NH):
        wout_toks.append(p.dma("gp", woutb[:, r, 0:H], woutT[128 * r:128 * (r + 1), :],
                               deps=abf2_guard, tag="wout"))
    att_sts = []
    ev_prev = {0: None, 1: None}
    for it in range(c.NTT):
        pp = it % 2
        ps = psA if pp == 0 else psB
        mm_last = None
        for r in range(c.NH):
            d = []
            if r == 0:
                d = wout_toks + [b2[ss]["gh"] for ss in range(c.NH)] + [ev_prev[pp]]
                if it < 2:
                    d = d + [prev_gy]
            for j in range(c.NH512):
                mm_last = p.emit("pe", (lambda E, r=r, j=j, it=it, ps=ps: E.matmul(
                    ps[:, 512 * j:512 * (j + 1)], gyh[:, r, 128 * it:128 * (it + 1)],
                    woutb[:, r, 512 * j:512 * (j + 1)],
                    start=(r == 0), stop=(r == c.NH - 1))), deps=d if j == 0 else [])
                d = []
        ev = p.emit("dv", lambda E, pp=pp, ps=ps: E.tensor_copy(attb[:, pp, :],
                                                                ps[:, 0:H]),
                    deps=[mm_last, att_sts[it - 2] if it >= 2 else None])
        ev_prev[pp] = ev
        att_sts.append(p.dma("ac", att_s[128 * it:128 * (it + 1), :], attb[:, pp, :],
                             deps=[ev], tag="att_st"))

    if stop_after == "B3":
        return _stop(p, nc, stack, att_sts)
    abf.reset(); af32.reset()
    abf.reset()
    abf.mark(); af32.mark()
    xqT = abf.alloc([c.NH, CT])
    gstr = abf.alloc([2, c.NH, c.ICH])
    ustr = abf.alloc([2, c.NH, c.ICH])
    hqb = abf.alloc([2, c.ICH])
    hsil = af32.alloc([c.ICH])
    hful = af32.alloc([c.ICH])
    hma = af32.alloc([c.ICH // 32])
    hs2 = af32.alloc([c.ICH // 32])
    hrs = af32.alloc([c.ICH // 32])
    hq8 = af8.alloc([c.ICH])

    c1_guard = ([b2[s]["gh"] for s in range(c.NH)] + att_sts
                + [ev_prev[0], ev_prev[1], wout_toks[-1], mm_last])
    xqT_toks = []
    for h in range(c.NH):
        xqT_toks.append(p.dma("ac", xqT[:, h, :], xnq_s[:, 128 * h:128 * (h + 1)],
                              deps=(a_done + c1_guard), transpose=True, tag="xqT"))

    GSW = min(c.ICH, ISH)
    def load_ffn_chunk(ch, bi, deps):
        toks = []
        for h in range(c.NH):
            for s0 in range(0, c.ICH, GSW):
                r0 = c.ICH * ch + s0
                rk, lc = r0 // ISH, r0 % ISH
                toks.append(p.dma("ac", gstr[:, bi, h, s0:s0 + GSW],
                                  gwT_view(rk, OFF_G)[128 * h:128 * (h + 1),
                                                      lc:lc + GSW],
                                  deps=deps, tag="gstr"))
                toks.append(p.dma("ac", ustr[:, bi, h, s0:s0 + GSW],
                                  gwT_view(rk, OFF_U)[128 * h:128 * (h + 1),
                                                      lc:lc + GSW],
                                  deps=deps, tag="ustr"))
        return toks

    ch_toks = {0: load_ffn_chunk(0, 0, wq_done + c1_guard)}
    if c.NICH > 1:
        ch_toks[1] = load_ffn_chunk(1, 1, wq_done + c1_guard)
    hq_sts = []
    ps_free = {0: ev_prev[0], 1: ev_prev[1]}
    prev_st = None
    prev_hm = None
    prev_q8 = None
    for ch in range(c.NICH):
        bi = ch % 2
        last_mms = []
        for it in range(c.NTT):
            pp = it % 2
            ps = psA if pp == 0 else psB
            mm_last = None
            d0 = []
            if it == 0:
                d0 = ch_toks[ch] + xqT_toks
            for h in range(c.NH):
                d = list(d0) if h == 0 else []
                if h == 0 and ps_free[pp] is not None:
                    d.append(ps_free[pp])
                mm_last = p.emit("pe", (lambda E, h=h, it=it, bi=bi, ps=ps: E.matmul(
                    ps[:, 0:c.ICH], xqT[:, h, 128 * it:128 * (it + 1)],
                    gstr[:, bi, h, :], start=(h == 0), stop=(h == c.NH - 1))), deps=d)
                mm_last = p.emit("pe", (lambda E, h=h, it=it, bi=bi, ps=ps: E.matmul(
                    ps[:, c.ICH:2 * c.ICH], xqT[:, h, 128 * it:128 * (it + 1)],
                    ustr[:, bi, h, :], start=(h == 0), stop=(h == c.NH - 1))), deps=[])
            last_mms.append(mm_last)
            sil = p.emit("ac", lambda E, ps=ps: E.activation(hsil, ps[:, 0:c.ICH],
                                                             AF.Silu),
                         deps=[mm_last, prev_hm], tag="silu")
            hm = p.emit("dv", lambda E, ps=ps: E.tensor_tensor(
                hful, hsil, ps[:, c.ICH:2 * c.ICH], ALU.mult),
                deps=[sil, prev_q8])
            prev_hm = hm
            ps_free[pp] = hm
            dq, q8t = quant_tile(hful, hq8, hma, hs2, hrs,
                                 hqb[:, it % 2, :], [hm, prev_st],
                                 dq_eng="gp")
            prev_q8 = q8t
            st = p.dma("ac", hq_s[128 * it:128 * (it + 1), c.ICH * ch:c.ICH * (ch + 1)],
                       hqb[:, it % 2, :], deps=[dq, prev_st], tag="hq_st")
            prev_st = st
            hq_sts.append(st)
        nxt = ch + 2
        if nxt < c.NICH:
            ch_toks[nxt] = load_ffn_chunk(nxt, nxt % 2, last_mms)

    if stop_after == "C1":
        return _stop(p, nc, stack, hq_sts)
    abf.reset(); af32.reset()
    abf.mark(); af32.mark()
    NHALF = 2 if H > 1024 else 1
    HW2 = H // NHALF
    NW512 = HW2 // 512
    hqT = abf.alloc([2, c.NIS, CT])
    dwT = abf.alloc([2, c.NIS, HW2])
    facc = abf.alloc([c.NTT, HW2])
    xres = af32.alloc([2, HW2])
    attr = af32.alloc([2, HW2])
    fout = af32.alloc([2, HW2])

    c2_guard = hq_sts + [t for t in xqT_toks]

    def load_sec(hh, sec, bi, deps):
        toks = []
        for ii in range(c.NIS):
            itile = sec * c.NIS + ii
            toks.append(p.dma("sp", hqT[:, bi, ii, :],
                              hq_s[:, 128 * itile:128 * (itile + 1)],
                              deps=deps, transpose=True, tag="hqT"))
            rk, lr = (128 * itile) // DSH, (128 * itile) % DSH
            toks.append(p.dma("ac", dwT[:, bi, ii, :],
                              dwT_view(rk)[lr:lr + 128, HW2 * hh:HW2 * (hh + 1)],
                              deps=deps, tag="dwT"))
        return toks

    hsecs = [(hh, sec) for hh in range(NHALF) for sec in range(c.NSEC)]
    sec_toks = {0: load_sec(hsecs[0][0], hsecs[0][1], 0, wq_done + c2_guard)}
    if len(hsecs) > 1:
        sec_toks[1] = load_sec(hsecs[1][0], hsecs[1][1], 1, wq_done + c2_guard)
    out_sts = []
    ev_prev = {0: None, 1: None}
    prev_out = {0: None, 1: None}
    for g, (hh, sec) in enumerate(hsecs):
        bi = g % 2
        if sec == 0:
            acc_tok = {}
        sec_last = []
        for it in range(c.NTT):
            pp = it % 2
            ps = psA if pp == 0 else psB
            mm_last = None
            for ii in range(c.NIS):
                d = []
                if ii == 0:
                    d = sec_toks[g] + [ev_prev[pp]]
                    if g == 0 and it < 2:
                        d.append(ps_free[pp])
                for j in range(NW512):
                    mm_last = p.emit("pe", (lambda E, ii=ii, j=j, it=it, bi=bi, ps=ps:
                                            E.matmul(
                        ps[:, 512 * j:512 * (j + 1)],
                        hqT[:, bi, ii, 128 * it:128 * (it + 1)],
                        dwT[:, bi, ii, 512 * j:512 * (j + 1)],
                        start=(ii == 0), stop=(ii == c.NIS - 1))),
                        deps=d if j == 0 else [])
                    d = []
            sec_last.append(mm_last)
            is_first, is_last = sec == 0, sec == c.NSEC - 1
            if not is_last:
                if is_first:
                    ev = p.emit("dv", lambda E, it=it, ps=ps: E.tensor_copy(
                        facc[:, it, :], ps[:, 0:HW2]), deps=[mm_last], tag="dacc0")
                else:
                    ev = p.emit("dv", lambda E, it=it, ps=ps: E.tensor_tensor(
                        facc[:, it, :], facc[:, it, :], ps[:, 0:HW2], ALU.add),
                        deps=[mm_last, acc_tok[it]], tag="dacc")
                acc_tok[it] = ev
                ev_prev[pp] = ev
            else:
                ldx = p.dma("ac", xres[:, pp, :],
                            x_loc[128 * it:128 * (it + 1), HW2 * hh:HW2 * (hh + 1)],
                            deps=[prev_out[pp]], tag="xres")
                lda = p.dma("ac", attr[:, pp, :],
                            att_s[128 * it:128 * (it + 1), HW2 * hh:HW2 * (hh + 1)],
                            deps=[att_sts[it], prev_out[pp]], tag="attr")
                if is_first:
                    e1 = p.emit("dv", lambda E, it=it, ps=ps, pp=pp: E.tensor_copy(
                        fout[:, pp, :], ps[:, 0:HW2]),
                        deps=[mm_last, prev_out[pp]], tag="fin1")
                else:
                    e1 = p.emit("dv", lambda E, it=it, ps=ps, pp=pp: E.tensor_tensor(
                        fout[:, pp, :], facc[:, it, :], ps[:, 0:HW2], ALU.add),
                        deps=[mm_last, acc_tok[it], prev_out[pp]], tag="fin1")
                e2 = p.emit("dv", lambda E, pp=pp: E.tensor_tensor(
                    fout[:, pp, :], fout[:, pp, :], attr[:, pp, :], ALU.add),
                    deps=[e1, lda], tag="fin2")
                e3 = p.emit("dv", lambda E, pp=pp: E.tensor_tensor(
                    fout[:, pp, :], fout[:, pp, :], xres[:, pp, :], ALU.add),
                    deps=[e2, ldx], tag="fin3")
                ev_prev[pp] = e3
                st = p.dma("ac",
                           out_d[128 * it:128 * (it + 1), HW2 * hh:HW2 * (hh + 1)],
                           fout[:, pp, :], deps=[e3], tag="out_st")
                prev_out[pp] = st
                out_sts.append(st)
        if g + 2 < len(hsecs):
            sec_toks[g + 2] = load_sec(hsecs[g + 2][0], hsecs[g + 2][1],
                                       g % 2, sec_last)

    if debug_dumps:
        dsem_toks = []
        for nm, t in [("d_xnbf", xnbf_s), ("d_xnq", xnq_s), ("d_att", att_s),
                      ("d_hq", hq_s), ("d_anx", anx_s), ("d_cc", cc_out),
                      ("d_wsh", wsh_out)]:
            od = nc.dram_tensor(nm, list(t.shape), t.dtype, kind="ExternalOutput").ap()
            dsem_toks.append(p.dma("gp", od, t, deps=out_sts, tag="dbg"))
        out_sts = out_sts + dsem_toks
    p.emit("gp", lambda E: E.engine_nop(), deps=out_sts, tag="fence")

    with nc.Block() as blk:
        p.finalize(nc, blk, stack)
    stack.close()
    return nc


def _stop(p, nc, stack, toks):
    p.emit("gp", lambda E: E.engine_nop(), deps=toks, tag="fence")
    with nc.Block() as blk:
        p.finalize(nc, blk, stack)
    stack.close()
    return nc



def make_inputs(cfg, inp):
    c = cfg
    x = np.ascontiguousarray(np.asarray(inp["L0_attn_x"], np.float32))
    W_xy = np.asarray(inp["L0_attn_W_xy"], np.float32)
    wi = np.asarray(inp["L0_attn_wi"], np.float32)
    bi = np.asarray(inp["L0_attn_bi"], np.float32)
    wa = np.asarray(inp["L0_attn_wa"], np.float32)
    ba = np.asarray(inp["L0_attn_ba"], np.float32)
    a_param = np.asarray(inp["L0_attn_a_param"], np.float32)
    conv_w = np.asarray(inp["L0_attn_conv_w"], np.float32)
    conv_b = np.asarray(inp["L0_attn_conv_b"], np.float32)
    W_out = np.asarray(inp["L0_attn_W_out"], np.float32)
    gw = np.asarray(inp["L0_ffn_gate_w"], np.float32)
    uw = np.asarray(inp["L0_ffn_up_w"], np.float32)
    dw = np.asarray(inp["L0_ffn_down_w"], np.float32)
    ISH = gw.shape[0] // NCORES
    DSH = dw.shape[1] // NCORES

    wxyT = np.ascontiguousarray(W_xy.T)
    woutT = np.ascontiguousarray(W_out.T)
    wi_r = np.ascontiguousarray(wi.transpose(1, 0, 2))
    wa_r = np.ascontiguousarray(wa.transpose(1, 0, 2))
    bi_r = np.ascontiguousarray(bi.reshape(c.NH, 128).T)
    ba_r = np.ascontiguousarray(ba.reshape(c.NH, 128).T)
    ap_r = np.ascontiguousarray(a_param.reshape(c.NH, 128).T)
    cw_r = np.ascontiguousarray(conv_w.T.reshape(c.NH, 128, 4).transpose(1, 0, 2))
    cb_r = np.ascontiguousarray(conv_b.reshape(c.NH, 128).T)
    consts = np.tile(np.array([[CONSTS[k] for k in CONSTS]], np.float32), (128, 1))

    maps = []
    for core in range(NCORES):
        b, k = core // c.NCHUNK, core % c.NCHUNK
        t0 = k * c.CT
        xp = np.zeros((128, c.H), np.float32)
        if k > 0:
            xp[0:3] = x[b, t0 - 3:t0]
        cmask = np.zeros((128, c.NH, 6), np.float32)
        if k > 0:
            cmask[:, :, 3 * b + (k - 1)] = 1.0
        maps.append(dict(
            x_loc=np.ascontiguousarray(x[b, t0:t0 + c.CT]), x_pre=xp,
            wxyT=wxyT, woutT=woutT, wi_in=wi_r, wa_in=wa_r, bi_in=bi_r, ba_in=ba_r,
            ap_in=ap_r, cw_in=cw_r, cb_in=cb_r,
            gw_sh=np.ascontiguousarray(gw[ISH * core:ISH * (core + 1)]),
            uw_sh=np.ascontiguousarray(uw[ISH * core:ISH * (core + 1)]),
            dw_sh=np.ascontiguousarray(dw[:, DSH * core:DSH * (core + 1)]),
            ident_in=np.eye(128, dtype=ml_dtypes.bfloat16),
            consts_in=consts, cmask_in=cmask,
        ))
    return maps


def kernel(**inputs):
    cfg = FULL
    nc = build_program(cfg)
    maps = make_inputs(cfg, inputs)
    res = run_bass_kernel_spmd(nc, maps, core_ids=list(range(NCORES)))
    out = np.empty((cfg.B, cfg.T, cfg.H), np.float32)
    for core in range(NCORES):
        b, k = core // cfg.NCHUNK, core % cfg.NCHUNK
        out[b, k * cfg.CT:(k + 1) * cfg.CT] = np.asarray(res.results[core]["out"])
    return out


# version 15
# speedup vs baseline: 1.0161x; 1.0161x over previous
import numpy as np
import ml_dtypes
import contextlib
import concourse.bass as bass
import concourse.mybir as mybir
from concourse.bass_utils import run_bass_kernel_spmd

F32 = mybir.dt.float32
F16 = mybir.dt.float16
BF16 = mybir.dt.bfloat16
FP8 = mybir.dt.float8e4
AF = mybir.ActivationFunctionType
ALU = mybir.AluOpType
AX = mybir.AxisListType

NCORES = 8


class Cfg:
    def __init__(self, B=2, T=8192, H=2048, I=8192):
        self.B, self.T, self.H, self.I = B, T, H, I
        self.COLS = 2 * H
        self.NCHUNK = NCORES // B
        self.CT = T // self.NCHUNK
        assert self.CT % 512 == 0 and H % 128 == 0 and I % 512 == 0
        self.NTT = self.CT // 128
        self.NH = H // 128
        self.NI = I // 128
        self.ICH = 512
        self.NICH = I // self.ICH
        self.SEC = 1024 if I % 1024 == 0 else I
        self.NSEC = I // self.SEC
        self.NIS = self.SEC // 128
        self.CTP = self.CT + 128
        self.NT512 = self.CT // 512
        self.NH512 = H // 512


FULL = Cfg()



class Prog:
    ENGS = ("pe", "dv", "ac", "gp", "sp")

    def __init__(self):
        self.ops = []
        self.n = {e: 0 for e in self.ENGS}

    def emit(self, eng, fn, deps=(), asyn=False, amt=1, tag=""):
        tok = (eng, self.n[eng])
        self.ops.append(dict(e=eng, i=self.n[eng], fn=fn,
                             deps=[d for d in deps if d is not None],
                             asyn=asyn, amt=amt, marked=False, tag=tag))
        self.n[eng] += 1
        return tok

    def dma(self, eng, out, in_, deps=(), transpose=False, tag=""):
        if transpose:
            eng = "sp"
            fn = (lambda E, o=out, i=in_: E.dma_start(out=o, in_=i, transpose=True))
        else:
            fn = (lambda E, o=out, i=in_: E.dma_start(out=o, in_=i))
        return self.emit(eng, fn, deps, asyn=True, amt=16, tag=tag)

    NPOOL = 8

    def finalize(self, nc, blk, stack):
        byeng = {e: [] for e in self.ENGS}
        for op in self.ops:
            byeng[op["e"]].append(op)
        idx = {(op["e"], op["i"]): op for op in self.ops}
        for op in self.ops:
            for d in op["deps"]:
                pr = idx[d]
                if pr["asyn"]:
                    continue
                if pr["e"] == "pe" and op["e"] == "pe":
                    continue
                pr["marked"] = True
        sems = {}
        pool = {}
        for e in self.ENGS:
            sems[e] = stack.enter_context(nc.semaphore(f"s_{e}"))
        for e in self.ENGS:
            aops = [op for op in byeng[e] if op["asyn"] and op["amt"] == 16]
            ccops = [op for op in byeng[e] if op["asyn"] and op["amt"] != 16]
            for j, op in enumerate(ccops):
                lane = self.NPOOL + j
                key = (e, lane)
                pool[key] = stack.enter_context(nc.semaphore(f"q_{e}cc{j}"))
                op["lane"] = lane
                op["chain_wait"] = 0
                op["val"] = op["amt"]
                op["marked"] = True
            lane_cnt = {}
            for j, op in enumerate(aops):
                lane = j % self.NPOOL
                key = (e, lane)
                if key not in pool:
                    pool[key] = stack.enter_context(nc.semaphore(f"q_{e}{lane}"))
                prev = lane_cnt.get(lane, 0)
                op["lane"] = lane
                op["chain_wait"] = prev
                op["val"] = prev + op["amt"]
                lane_cnt[lane] = op["val"]
                op["marked"] = True
            v = 0
            for op in byeng[e]:
                if op["asyn"]:
                    continue
                if op["marked"]:
                    v += op["amt"]
                op["val"] = v

        def mk(e):
            def run(E):
                last = {}
                for op in byeng[e]:
                    wants = {}
                    for d in op["deps"]:
                        pr = idx[d]
                        if pr["asyn"]:
                            k = (pr["e"], pr["lane"])
                            wants[k] = max(wants.get(k, 0), pr["val"])
                        elif pr["e"] == "pe" and e == "pe":
                            continue
                        else:
                            wants[pr["e"]] = max(wants.get(pr["e"], 0), pr["val"])
                    if op.get("asyn") and op["chain_wait"] > 0:
                        k = (e, op["lane"])
                        wants[k] = max(wants.get(k, 0), op["chain_wait"])
                    for k, v in wants.items():
                        if last.get(k, 0) < v:
                            E.wait_ge(pool[k] if isinstance(k, tuple) else sems[k], v)
                            last[k] = v
                    ins = op["fn"](E)
                    if op["asyn"]:
                        ins.then_inc(pool[(e, op["lane"])], op["amt"])
                    elif op["marked"]:
                        ins.then_inc(sems[e], op["amt"])
            return run

        blk.tensor(mk("pe"))
        blk.vector(mk("dv"))
        blk.scalar(mk("ac"))
        blk.gpsimd(mk("gp"))
        blk.sync(mk("sp"))


class Arena:

    def __init__(self, nc, stack, name, nelem, dt, **kw):
        self.t = stack.enter_context(nc.sbuf_tensor(name, [128, nelem], dt, **kw))
        self.nelem = nelem
        self.off = 0
        self.marks = []

    def alloc(self, shape):
        n = 1
        for s in shape:
            n *= s
        o = (self.off + 63) // 64 * 64
        self.off = o + n
        assert self.off <= self.nelem, f"arena overflow {self.off} > {self.nelem}"
        ap = self.t.ap()[:, o:o + n]
        if len(shape) > 1:
            kw = {}
            names = "abcd"[: len(shape)]
            pat = f"p ({' '.join(names)}) -> p {' '.join(names)}"
            kw = {names[j]: shape[j] for j in range(1, len(shape))}
            ap = ap.rearrange(pat, **kw)
        return ap

    def mark(self):
        self.marks.append(self.off)

    def reset(self):
        self.off = self.marks.pop()


CONSTS = {"one": 1.0, "half": 0.5}
CIDX = {k: i for i, k in enumerate(CONSTS)}


def build_program(cfg: Cfg, stop_after=None, debug_dumps=False):
    nc = bass.Bass("TRN2", target_bir_lowering=False)
    c = cfg
    H, I, CT = c.H, c.I, c.CT
    stack = contextlib.ExitStack()
    p = Prog()

    def din(name, shape, dt=F32):
        return nc.dram_tensor(name, shape, dt, kind="ExternalInput").ap()

    x_loc = din("x_loc", [CT, H])
    x_pre = din("x_pre", [128, H])
    wxyT = din("wxyT", [H, c.COLS])
    woutT = din("woutT", [H, H])
    wi_in = din("wi_in", [128, c.NH, 128])
    wa_in = din("wa_in", [128, c.NH, 128])
    bi_in = din("bi_in", [128, c.NH])
    ba_in = din("ba_in", [128, c.NH])
    ap_in = din("ap_in", [128, c.NH])
    cw_in = din("cw_in", [128, c.NH, 4])
    cb_in = din("cb_in", [128, c.NH])
    ISH = I // NCORES
    DSH = I // NCORES
    gw_in = din("gw_sh", [ISH, H])
    uw_in = din("uw_sh", [ISH, H])
    dw_in = din("dw_sh", [H, DSH])
    ident_in = din("ident_in", [128, 128], BF16)
    consts_in = din("consts_in", [128, len(CONSTS)])
    cmask_in = din("cmask_in", [128, c.NH, 6])
    out_d = nc.dram_tensor("out", [CT, H], F32, kind="ExternalOutput").ap()

    def dscr(name, shape, dt, **kw):
        return nc.dram_tensor(name, shape, dt, kind="Internal", **kw).ap()

    xnbf_s = dscr("xnbf_s", [c.CTP, H], BF16)
    xnq_s = dscr("xnq_s", [CT, H], BF16)
    WSH = ISH * H + ISH * H + H * DSH
    OFF_G, OFF_U, OFF_D = 0, ISH * H, 2 * ISH * H
    anx_s = dscr("anx_s", [2, c.NH, 128, CT], BF16)
    hq_s = dscr("hq_s", [CT, I], BF16)
    att_s = dscr("att_s", [CT, H], F32)
    from concourse.replica_groups import maybe_share_collective_output_space
    RG = [list(range(NCORES))]
    ccsp = maybe_share_collective_output_space("AllGather", RG, prefer_local=True)
    cc_in = dscr("cc_in", [2, 128, c.NH], F32)
    cc_out = dscr("cc_out", [NCORES * 2, 128, c.NH], F32, addr_space=ccsp)
    wsh_in = dscr("wsh_in", [1, WSH], BF16)
    wsh_out = dscr("wsh_out", [NCORES, WSH], BF16, addr_space=ccsp)

    def gwT_view(rk, off):
        return wsh_out[rk:rk + 1, off:off + ISH * H].rearrange(
            "q (a b) -> (q a) b", b=ISH)

    def dwT_view(rk):
        return wsh_out[rk:rk + 1, OFF_D:OFF_D + H * DSH].rearrange(
            "q (a b) -> (q a) b", b=H)

    abf = Arena(nc, stack, "abf", 78 * 1024, BF16, is_transpose_dest=True)
    af32 = Arena(nc, stack, "af32", 10 * 1024, F32)
    af8 = Arena(nc, stack, "af8", 5 * 1024, FP8)

    csts = af32.alloc([len(CONSTS)])
    bi_sb = af32.alloc([c.NH])
    ba_sb = af32.alloc([c.NH])
    msp_sb = af32.alloc([c.NH])
    cw_sb = af32.alloc([c.NH, 4])
    cb_sb = af32.alloc([c.NH])
    cmask_sb = af32.alloc([c.NH, 6])
    tiny = af32.alloc([2, max(16, c.NH)])
    hfin = af32.alloc([c.NH])
    Ach = af32.alloc([c.NH])
    ssum = af32.alloc([c.NH])
    h_in = af32.alloc([c.NH])
    gath = af32.alloc([16, c.NH])
    cand = af32.alloc([6, c.NH])
    wi_sb = abf.alloc([c.NH, 128])
    wa_sb = abf.alloc([c.NH, 128])

    CST = lambda name: csts[:, CIDX[name]:CIDX[name] + 1]

    ld_csts = p.dma("sp", csts, consts_in, tag="csts")
    ld_wi = p.dma("gp", wi_sb, wi_in, tag="wi")
    ld_wa = p.dma("gp", wa_sb, wa_in, tag="wa")
    ld_bi = p.dma("sp", bi_sb, bi_in)
    ld_ba = p.dma("sp", ba_sb, ba_in)
    ld_ap = p.dma("sp", msp_sb, ap_in)
    ld_cw = p.dma("sp", cw_sb, cw_in)
    ld_cb = p.dma("sp", cb_sb, cb_in)
    ld_cm = p.dma("sp", cmask_sb, cmask_in)

    t_e = p.emit("ac", lambda E: E.activation(tiny[:, 0, 0:c.NH], msp_sb, AF.Exp),
                 deps=[ld_ap])
    t_l = p.emit("ac", lambda E: E.activation(tiny[:, 1, 0:c.NH], tiny[:, 0, 0:c.NH],
                                              AF.Ln, bias=CST("one")),
                 deps=[t_e, ld_csts])
    t_m = p.emit("dv", lambda E: E.tensor_scalar_mul(msp_sb, tiny[:, 1, 0:c.NH], -8.0),
                 deps=[t_l])

    psA = stack.enter_context(nc.psum_tensor("psA", [128, 2048], F32))
    psB = stack.enter_context(nc.psum_tensor("psB", [128, 2048], F32))

    def quant_tile(x_ap, q8_ap, ma, s2, rs, outbf_ap, deps, dq_eng="gp"):
        feat = x_ap.shape[-1]
        nb = feat // 32
        x3 = x_ap.rearrange("p (a b) -> p a b", b=32)
        t_ma = p.emit("dv", lambda E: E.tensor_reduce(ma[:, 0:nb], x3, AX.X, ALU.max,
                                                      apply_absolute_value=True), deps=deps)
        t_s2 = p.emit("dv", lambda E: E.tensor_scalar(s2[:, 0:nb], ma[:, 0:nb],
                                                      1.0 / 224.0, 2e-30, ALU.mult,
                                                      ALU.max), deps=[t_ma])
        t_rs = p.emit("dv", lambda E: E.reciprocal(rs[:, 0:nb], s2[:, 0:nb]), deps=[t_s2])
        q83 = q8_ap.rearrange("p (a b) -> p a b", b=32)
        t_q8 = p.emit("dv", lambda E: E.tensor_tensor(
            q83, x3, rs[:, 0:nb].unsqueeze(2).broadcast_to([128, nb, 32]), ALU.mult),
            deps=[t_rs])
        o3 = outbf_ap.rearrange("p (a b) -> p a b", b=32)
        t_dq = p.emit(dq_eng, lambda E: E.tensor_tensor(
            o3, q83, s2[:, 0:nb].unsqueeze(2).broadcast_to([128, nb, 32]), ALU.mult),
            deps=[t_q8])
        return t_dq, t_q8

    af32.mark(); abf.mark(); af8.mark()
    WF = min(2048, max(H, 2048))
    NWB = 2
    wq_x = af32.alloc([NWB, WF])
    wq_bf = abf.alloc([NWB, WF])
    wq_q8 = af8.alloc([1, WF])
    wq_ma = af32.alloc([NWB, WF // 32])
    wq_s2 = af32.alloc([NWB, WF // 32])
    wq_rs = af32.alloc([NWB, WF // 32])

    wqT_bf = abf.alloc([NWB, WF // 128, 128])
    ident = abf.alloc([128])
    ld_id = p.dma("gp", ident, ident_in, tag="ident")

    _wst = dict(terms={}, trp={}, q8prev={}, n=0)

    def emit_w_tile(srcw, dstT, r, feat, deps0=()):
        i = _wst["n"]; _wst["n"] += 1
        bi = i % NWB
        qi = 0
        ld = p.dma("ac", wq_x[:, bi, 0:feat], srcw[128 * r:128 * (r + 1), :],
                   deps=list(deps0) + [_wst["terms"].get(bi)], tag="wq_ld")
        dq, q8t = quant_tile(wq_x[:, bi, 0:feat], wq_q8[:, qi, 0:feat],
                             wq_ma[:, bi, :], wq_s2[:, bi, :], wq_rs[:, bi, :],
                             wq_bf[:, bi, 0:feat],
                             [ld, _wst["q8prev"].get(qi)], dq_eng="gp")
        _wst["q8prev"][qi] = dq
        evs = []
        for j in range(feat // 128):
            sl = (j % 4) * 512
            tr = p.emit("pe", (lambda E, bi=bi, j=j, sl=sl: E.transpose(
                psA[:, sl:sl + 64].bitcast(BF16),
                wq_bf[:, bi, 128 * j:128 * (j + 1)], ident)),
                deps=[dq, ld_id, _wst["trp"].get(j % 4)], tag="wtr")
            ev = p.emit("dv", (lambda E, bi=bi, j=j, sl=sl: E.tensor_copy(
                wqT_bf[:, bi, j, :], psA[:, sl:sl + 64].bitcast(BF16))),
                deps=[tr, _wst["terms"].get(bi)], tag="wtr_ev")
            _wst["trp"][j % 4] = ev
            evs.append(ev)
        st = p.dma("ac",
                   dstT[:, 128 * r:128 * (r + 1)].rearrange(
                       "(j q) c -> q j c", q=128),
                   wqT_bf[:, bi, 0:feat // 128, :], deps=evs, tag="wq_st")
        _wst["terms"][bi] = st
        return st

    gq_view = wsh_in[0:1, OFF_G:OFF_G + ISH * H].rearrange("q (a b) -> (q a) b", b=ISH)
    uq_view = wsh_in[0:1, OFF_U:OFF_U + ISH * H].rearrange("q (a b) -> (q a) b", b=ISH)
    dq_view = wsh_in[0:1, OFF_D:OFF_D + H * DSH].rearrange("q (a b) -> (q a) b", b=H)
    w_tiles = ([(gw_in, gq_view, r, H) for r in range(ISH // 128)]
               + [(uw_in, uq_view, r, H) for r in range(ISH // 128)]
               + [(dw_in, dq_view, r, DSH) for r in range(H // 128)])
    NAB = 2
    ax = abf.alloc([NAB, 2 * H]).bitcast(F32)
    axn = abf.alloc([NAB, 2 * H]).bitcast(F32)
    axnb = abf.alloc([NAB, H])
    axqb = abf.alloc([NAB, H])
    ast = af32.alloc([NAB, 8])
    ama = af32.alloc([NAB, H // 32])
    as2 = af32.alloc([NAB, H // 32])
    ars = af32.alloc([NAB, H // 32])
    axq8 = af8.alloc([1, H])
    _ast = dict(q8prev={}, n=0)

    def phaseA_tile(src_rows, dst_bf, dst_q, bi, deps_in):
        ld = p.dma("ac", ax[:, bi, :], src_rows, deps=deps_in, tag="A_ld")
        s1 = p.emit("dv", lambda E: E.reduce_sum(ast[:, bi, 0:1], ax[:, bi, :], AX.X),
                    deps=[ld])
        ssq = p.emit("ac", lambda E: E.activation(axn[:, bi, :], ax[:, bi, :],
                                                  AF.Square,
                                                  accum_out=ast[:, bi, 2:3]),
                     deps=[ld], tag="ssq")
        m = p.emit("dv", lambda E: E.tensor_scalar_mul(ast[:, bi, 1:2], ast[:, bi, 0:1],
                                                       1.0 / H), deps=[s1])
        m2 = p.emit("dv", lambda E: E.tensor_tensor(ast[:, bi, 3:4], ast[:, bi, 1:2],
                                                    ast[:, bi, 1:2], ALU.mult), deps=[m])
        m2e = p.emit("dv", lambda E: E.tensor_scalar(ast[:, bi, 4:5], ast[:, bi, 3:4],
                                                     1.0, 1e-5, ALU.mult, ALU.subtract),
                     deps=[m2])
        var = p.emit("dv", lambda E: E.tensor_scalar(ast[:, bi, 5:6], ast[:, bi, 2:3],
                                                     1.0 / H, ast[:, bi, 4:5], ALU.mult,
                                                     ALU.subtract), deps=[ssq, m2e])
        r1 = p.emit("dv", lambda E: E.reciprocal(ast[:, bi, 6:7], ast[:, bi, 5:6]),
                    deps=[var])
        rstd = p.emit("ac", lambda E: E.activation(ast[:, bi, 7:8], ast[:, bi, 6:7],
                                                   AF.Sqrt), deps=[r1])
        xn = p.emit("dv", lambda E: E.tensor_scalar(axn[:, bi, :], ax[:, bi, :],
                                                    ast[:, bi, 1:2], ast[:, bi, 7:8],
                                                    ALU.subtract, ALU.mult),
                    deps=[rstd, ssq], tag="xn")
        xnb = p.emit("ac", lambda E: E.activation(axnb[:, bi, :], axn[:, bi, :],
                                                  AF.Copy), deps=[xn], tag="xnb")
        stb = p.dma("gp", dst_bf, axnb[:, bi, :], deps=[xnb], tag="A_stb")
        done = [stb]
        if dst_q is not None:
            qi = 0
            _ast["n"] += 1
            dq, _ = quant_tile(axn[:, bi, :], axq8[:, qi, :], ama[:, bi, :],
                               as2[:, bi, :], ars[:, bi, :],
                               axqb[:, bi, :],
                               [xn, _ast["q8prev"].get(qi)], dq_eng="gp")
            _ast["q8prev"][qi] = dq
            stq = p.dma("gp", dst_q, axqb[:, bi, :], deps=[dq], tag="A_stq")
            done.append(stq)
        return done

    a_done_tiles = []
    wq_sts = []
    a_idx = 0
    nW, nA = len(w_tiles), c.NTT + 1

    def emit_a(i):
        bi = i % NAB
        deps_in = a_done_tiles[i - NAB] if i >= NAB else []
        if i < c.NTT:
            a_done_tiles.append(phaseA_tile(
                x_loc[128 * i:128 * (i + 1), :],
                xnbf_s[128 * i:128 * (i + 1), :],
                xnq_s[128 * i:128 * (i + 1), :], bi, deps_in))
        else:
            a_done_tiles.append(phaseA_tile(x_pre, xnbf_s[CT:CT + 128, :], None,
                                            bi, deps_in))

    for wi_, (srcw, dstT, r, feat) in enumerate(w_tiles):
        wq_sts.append(emit_w_tile(srcw, dstT, r, feat,
                                  deps0=[t_m] if wi_ == 0 else []))
        while a_idx < nA and a_idx * nW <= (wi_ + 1) * nA:
            emit_a(a_idx)
            a_idx += 1
    while a_idx < nA:
        emit_a(a_idx)
        a_idx += 1
    a_done = [t for tile in a_done_tiles for t in tile]

    ccw = p.emit("gp", lambda E: E.collective_compute(
        "AllGather", ALU.bypass, RG, [wsh_in.flatten()], [wsh_out.flatten()]),
        deps=wq_sts, asyn=True, amt=1, tag="allgather_w")
    wq_done = [ccw]
    af32.reset(); abf.reset(); af8.reset()
    if stop_after == "A":
        return _stop(p, nc, stack, a_done)

    abf.mark()
    xnT = abf.alloc([c.NH, CT])
    wxy_pan = abf.alloc([2, c.NH, 128])
    xnT_toks = []
    for h in range(c.NH):
        xnT_toks.append(p.dma("sp", xnT[:, h, :],
                              xnbf_s[0:CT, 128 * h:128 * (h + 1)],
                              deps=a_done + wq_sts, transpose=True, tag="xnT"))

    abf.mark(); af32.mark()
    xnTp = abf.alloc([c.NH, 128])
    for h in range(c.NH):
        xnT_toks.append(p.dma("sp", xnTp[:, h, :],
                              xnbf_s[CT:c.CTP, 128 * h:128 * (h + 1)],
                              deps=a_done, transpose=True, tag="xnTp"))
    xp = abf.alloc([CT + 4])
    cacc = abf.alloc([2, CT])
    xr2 = abf.alloc([2, CT])
    gx2 = abf.alloc([2, CT])
    ga2 = abf.alloc([2, CT])
    ab2 = abf.alloc([2, CT])
    nx2 = abf.alloc([2, CT])
    loga = af32.alloc([2, CT])
    sscr = af32.alloc([CT])
    mscr = af32.alloc([CT])

    def load_wxy_panel(colt, bi, deps):
        src = wxyT[:, 128 * colt:128 * (colt + 1)].rearrange("(a q) e -> q a e", q=128)
        return p.dma("gp", wxy_pan[:, bi, :, :], src, deps=deps, tag="wxy")

    def b1_stage1(s, pan_tok, gi, prev):
        bi = s % 2
        mm_last = None
        for h in range(c.NH):
            d = []
            if h == 0:
                d = [pan_tok] + xnT_toks + [prev.get("psA"), prev.get("slot1")]
            for j in range(c.NT512):
                mm_last = p.emit("pe", (lambda E, h=h, j=j, bi=bi: E.matmul(
                    psA[:, 512 * j:512 * (j + 1)], wxy_pan[:, bi, h, :],
                    xnT[:, h, 512 * j:512 * (j + 1)],
                    start=(h == 0), stop=(h == c.NH - 1))),
                    deps=d if j == 0 else [])
                d = []
            mm_last = p.emit("pe", (lambda E, h=h, bi=bi: E.matmul(
                psB[:, 0:3], wxy_pan[:, bi, h, :], xnTp[:, h, 0:3],
                start=(h == 0), stop=(h == c.NH - 1))),
                deps=[prev.get("psB")] if h == 0 else [])
        evp = p.emit("dv", lambda E: E.tensor_copy(xp[:, 0:3], psB[:, 0:3]),
                     deps=[mm_last, prev.get("xp")])
        evx = p.emit("dv", lambda E: E.tensor_copy(xp[:, 3:3 + CT], psA[:, 0:CT]),
                     deps=[mm_last])
        c0 = p.emit("dv", lambda E, s=s: E.tensor_scalar(
            cacc[:, 0, :], xp[:, 0:CT], cw_sb[:, s, 0:1], cb_sb[:, s:s + 1],
            ALU.mult, ALU.add), deps=[evp, evx, ld_cw, ld_cb, prev.get("cacc")])
        c1 = p.emit("dv", lambda E, s=s: E.scalar_tensor_tensor(
            cacc[:, 1, :], xp[:, 1:1 + CT], cw_sb[:, s, 1:2], cacc[:, 0, :],
            ALU.mult, ALU.add), deps=[c0])
        c2 = p.emit("dv", lambda E, s=s: E.scalar_tensor_tensor(
            cacc[:, 0, :], xp[:, 2:2 + CT], cw_sb[:, s, 2:3], cacc[:, 1, :],
            ALU.mult, ALU.add), deps=[c1])
        c3 = p.emit("dv", lambda E, s=s, gi=gi: E.scalar_tensor_tensor(
            xr2[:, gi, :], xp[:, 3:3 + CT], cw_sb[:, s, 3:4], cacc[:, 0, :],
            ALU.mult, ALU.add), deps=[c2, prev.get("slot2")])
        gmm = []
        for j in range(c.NT512):
            m1 = p.emit("pe", (lambda E, j=j, s=s, gi=gi: E.matmul(
                psA[:, 512 * j:512 * (j + 1)], wi_sb[:, s, :],
                xr2[:, gi, 512 * j:512 * (j + 1)], start=True, stop=True)),
                deps=[c3, evx, ld_wi] if j == 0 else [])
            m2 = p.emit("pe", (lambda E, j=j, s=s, gi=gi: E.matmul(
                psB[:, 512 * j:512 * (j + 1)], wa_sb[:, s, :],
                xr2[:, gi, 512 * j:512 * (j + 1)], start=True, stop=True)),
                deps=[evp, ld_wa] if j == 0 else [])
            gmm += [m1, m2]
        sx = p.emit("ac", lambda E, s=s, gi=gi: E.activation(
            gx2[:, gi, :], psA[:, 0:CT], AF.Sigmoid, bias=bi_sb[:, s:s + 1]),
            deps=gmm + [ld_bi, prev.get("slot3")], tag="sigmoid")
        sa = p.emit("ac", lambda E, s=s, gi=gi: E.activation(
            ga2[:, gi, :], psB[:, 0:CT], AF.Sigmoid, bias=ba_sb[:, s:s + 1]),
            deps=gmm + [ld_ba, prev.get("slot4")], tag="sigmoid")
        return dict(sx=sx, sa=sa, xr=c3, xp_last=c3, mm_last=mm_last,
                    cacc_last=c3, evp=evp)

    def b1_stage2(s, gi, st1, prev2):
        la = p.emit("dv", lambda E, gi=gi, s=s: E.tensor_scalar_mul(
            loga[:, gi, :], ga2[:, gi, :], msp_sb[:, s:s + 1]), deps=[st1["sa"], t_m])
        a_b = p.emit("ac", lambda E, gi=gi: E.activation(ab2[:, gi, :], loga[:, gi, :],
                                                         AF.Exp),
                     deps=[la, prev2.get("a_st"), prev2.get("scan")], tag="lnexp")
        aq = p.emit("dv", lambda E, gi=gi: E.tensor_tensor(sscr[:], ab2[:, gi, :],
                                                           ab2[:, gi, :], ALU.mult),
                    deps=[a_b, prev2.get("nx")])
        m1 = p.emit("dv", lambda E: E.tensor_scalar(mscr[:], sscr[:], -1.0, 1.0,
                                                    ALU.mult, ALU.add),
                    deps=[aq, prev2.get("hf")])
        ln = p.emit("ac", lambda E: E.activation(sscr[:], mscr[:], AF.Ln),
                    deps=[m1], tag="lnexp")
        ex = p.emit("ac", lambda E: E.activation(mscr[:], sscr[:], AF.Exp,
                                                 scale=CST("half")),
                    deps=[ln, ld_csts], tag="lnexp")
        n1 = p.emit("dv", lambda E, gi=gi: E.tensor_tensor(sscr[:], xr2[:, gi, :],
                                                           gx2[:, gi, :], ALU.mult),
                    deps=[st1["xr"], st1["sx"], ex])
        nx = p.emit("dv", lambda E, gi=gi: E.tensor_tensor(nx2[:, gi, :], sscr[:],
                                                           mscr[:], ALU.mult),
                    deps=[n1, prev2.get("n_st"), prev2.get("scan")])
        asum = p.emit("dv", lambda E, gi=gi, s=s: E.reduce_sum(ssum[:, s:s + 1],
                                                               loga[:, gi, :], AX.X),
                      deps=[la])
        aex = p.emit("ac", lambda E, s=s: E.activation(Ach[:, s:s + 1], ssum[:, s:s + 1],
                                                       AF.Exp), deps=[asum], tag="lnexp")
        sc = p.emit("dv", lambda E, gi=gi: E.tensor_tensor_scan(
            mscr[:], ab2[:, gi, :], nx2[:, gi, :], 0.0, ALU.mult, ALU.add),
            deps=[nx], tag="scan1")
        hf = p.emit("dv", lambda E, s=s: E.tensor_copy(hfin[:, s:s + 1],
                                                       mscr[:, CT - 1:CT]), deps=[sc])
        sta = p.dma("gp", anx_s[0, s], ab2[:, gi, :], deps=[a_b], tag="anx_a")
        stn = p.dma("gp", anx_s[1, s], nx2[:, gi, :], deps=[nx], tag="anx_n")
        return dict(a_st=sta, n_st=stn, hf=hf, aex=aex, la=la, nx=nx, scan=sc)

    GROUP = 2
    pan_toks = {}
    pan_free = {0: None, 1: None}
    for s0 in range(min(2, c.NH)):
        pan_toks[s0] = load_wxy_panel(s0, s0 % 2, [])
    st1_prev = {"sx": wq_sts[-1], "sa": wq_sts[-1]}
    st2_prev = {0: {}, 1: {}}
    strip2 = {}
    s_list = list(range(c.NH))
    for g0 in range(0, c.NH, GROUP):
        grp = s_list[g0:g0 + GROUP]
        st1s = {}
        for gi, s in enumerate(grp):
            prev = dict(
                psA=st1_prev.get("sx"), psB=st1_prev.get("sa"),
                xp=st1_prev.get("xp_last"), cacc=st1_prev.get("cacc_last"),
                slot1=st2_prev[gi].get("la"),
                slot2=st2_prev[gi].get("nx"),
                slot3=st2_prev[gi].get("nx"),
                slot4=st2_prev[gi].get("la"),
            )
            st1s[gi] = b1_stage1(s, pan_toks[s], gi, prev)
            pan_free[s % 2] = st1s[gi]["mm_last"]
            st1_prev = st1s[gi]
            if (s + 2) < c.NH and (s + 2) not in pan_toks:
                pan_toks[s + 2] = load_wxy_panel(s + 2, (s + 2) % 2,
                                                 [pan_free[(s + 2) % 2]])
        for gi, s in enumerate(grp):
            st2 = b1_stage2(s, gi, st1s[gi], st2_prev[gi])
            st2_prev[gi] = st2
            strip2[s] = st2

    if stop_after == "B1":
        return _stop(p, nc, stack,
                     [strip2[s]["hf"] for s in range(c.NH)]
                     + [strip2[s]["a_st"] for s in range(c.NH)]
                     + [strip2[s]["n_st"] for s in range(c.NH)]
                     + [strip2[s]["aex"] for s in range(c.NH)])
    hA_st1 = p.dma("ac", cc_in[0], hfin, deps=[strip2[s]["hf"] for s in range(c.NH)],
                   tag="cc_st")
    hA_st2 = p.dma("ac", cc_in[1], Ach, deps=[strip2[s]["aex"] for s in range(c.NH)],
                   tag="cc_st")
    cc = p.emit("gp", lambda E: E.collective_compute(
        "AllGather", ALU.bypass, RG, [cc_in.flatten()], [cc_out.flatten()]),
        deps=[hA_st1, hA_st2], asyn=True, amt=1, tag="allgather")
    ld_g = p.dma("ac", gath, cc_out.rearrange("r p n -> p r n"), deps=[cc], tag="cc_ld")
    cand_toks = []
    for b in range(c.B):
        rb = b * c.NCHUNK
        t0 = p.emit("dv", lambda E, b=b, rb=rb: E.tensor_copy(
            cand[:, 3 * b + 0, :], gath[:, 2 * rb + 0, :]), deps=[ld_g])
        prevt = t0
        for kk in range(1, 3):
            r = rb + kk
            tm = p.emit("dv", lambda E, b=b, kk=kk, r=r: E.tensor_tensor(
                tiny[:, 0, 0:c.NH], gath[:, 2 * r + 1, :],
                cand[:, 3 * b + kk - 1, :], ALU.mult), deps=[prevt])
            prevt = p.emit("dv", lambda E, b=b, kk=kk, r=r: E.tensor_tensor(
                cand[:, 3 * b + kk, :], tiny[:, 0, 0:c.NH], gath[:, 2 * r + 0, :],
                ALU.add), deps=[tm])
        cand_toks.append(prevt)
    hz = p.emit("dv", lambda E: E.memset(h_in, 0.0), deps=[])
    prevt = hz
    for j in range(6):
        tm = p.emit("dv", lambda E, j=j: E.tensor_tensor(
            tiny[:, 1, 0:c.NH], cand[:, j, :], cmask_sb[:, :, j], ALU.mult),
            deps=cand_toks + [ld_cm, prevt])
        prevt = p.emit("dv", lambda E: E.tensor_tensor(
            h_in, h_in, tiny[:, 1, 0:c.NH], ALU.add), deps=[tm])
    hin_tok = prevt

    if stop_after == "CC":
        return _stop(p, nc, stack, [hin_tok])
    abf.reset(); af32.reset()
    abf.mark(); af32.mark()
    a_rb = af32.alloc([CT]).bitcast(BF16).rearrange("p (a b) -> p a b", a=2)
    n_rb = af32.alloc([CT]).bitcast(BF16).rearrange("p (a b) -> p a b", a=2)
    gyh = abf.alloc([c.NH, CT])
    hstr = af32.alloc([CT])

    b2_guard = ([strip2[s]["a_st"] for s in range(c.NH)]
                + [strip2[s]["n_st"] for s in range(c.NH)]
                + [strip2[c.NH - 1]["nx"], strip2[max(0, c.NH - 2)]["nx"]])
    prev_gy = None
    for s in range(c.NH):
        bi = s % 2
        ypan = load_wxy_panel(c.NH + s, bi, [pan_free[bi]])
        mm_last = None
        for h in range(c.NH):
            d = []
            if h == 0:
                d = [ypan, prev_gy]
                if s == 0:
                    d += [st1_prev["sx"]] + b2_guard
            for j in range(c.NT512):
                mm_last = p.emit("pe", (lambda E, h=h, j=j, bi=bi: E.matmul(
                    psA[:, 512 * j:512 * (j + 1)], wxy_pan[:, bi, h, :],
                    xnT[:, h, 512 * j:512 * (j + 1)],
                    start=(h == 0), stop=(h == c.NH - 1))), deps=d if j == 0 else [])
                d = []
        pan_free[bi] = mm_last
        gy = p.emit("ac", lambda E, s=s: E.activation(gyh[:, s, :], psA[:, 0:CT],
                                                      AF.Gelu),
                    deps=[mm_last, prev_gy], tag="gelu")
        prev_gy = gy
    last_gy = prev_gy
    b2 = {}
    prev_gh = None
    rb_toks = {}
    for s in range(min(2, c.NH)):
        rb_toks[s] = (p.dma("ac", a_rb[:, s % 2, :], anx_s[0, s],
                            deps=b2_guard + [strip2[s]["a_st"]], tag="a_rb"),
                      p.dma("ac", n_rb[:, s % 2, :], anx_s[1, s],
                            deps=[strip2[s]["n_st"]], tag="n_rb"))
    for s in range(c.NH):
        bi = s % 2
        at, ntk = rb_toks[s]
        sc = p.emit("dv", lambda E, s=s, bi=bi: E.tensor_tensor_scan(
            hstr, a_rb[:, bi, :], n_rb[:, bi, :], h_in[:, s:s + 1],
            ALU.mult, ALU.add),
            deps=[at, ntk, hin_tok, prev_gh], tag="scan2")
        gh = p.emit("dv", lambda E, s=s: E.tensor_tensor(gyh[:, s, :], gyh[:, s, :],
                                                         hstr, ALU.mult),
                    deps=[sc, last_gy], tag="gh")
        prev_gh = gh
        b2[s] = dict(gh=gh)
        if s + 2 < c.NH:
            rb_toks[s + 2] = (p.dma("ac", a_rb[:, bi, :], anx_s[0, s + 2],
                                    deps=[sc, strip2[s + 2]["a_st"]], tag="a_rb"),
                              p.dma("ac", n_rb[:, bi, :], anx_s[1, s + 2],
                                    deps=[sc, strip2[s + 2]["n_st"]], tag="n_rb"))

    if stop_after == "B2":
        return _stop(p, nc, stack, [b2[s]["gh"] for s in range(c.NH)])
    abf2_guard = [b2[s]["gh"] for s in range(c.NH)] + [pan_free[0], pan_free[1]]
    woutb = xnT
    af32.reset(); af32.mark()
    attb = af32.alloc([2, H])
    wout_toks = []
    for r in range(c.NH):
        wout_toks.append(p.dma("gp", woutb[:, r, 0:H], woutT[128 * r:128 * (r + 1), :],
                               deps=abf2_guard, tag="wout"))
    att_sts = []
    ev_prev = {0: None, 1: None}
    for it in range(c.NTT):
        pp = it % 2
        ps = psA if pp == 0 else psB
        mm_last = None
        for r in range(c.NH):
            d = []
            if r == 0:
                d = wout_toks + [b2[ss]["gh"] for ss in range(c.NH)] + [ev_prev[pp]]
                if it < 2:
                    d = d + [prev_gy]
            for j in range(c.NH512):
                mm_last = p.emit("pe", (lambda E, r=r, j=j, it=it, ps=ps: E.matmul(
                    ps[:, 512 * j:512 * (j + 1)], gyh[:, r, 128 * it:128 * (it + 1)],
                    woutb[:, r, 512 * j:512 * (j + 1)],
                    start=(r == 0), stop=(r == c.NH - 1))), deps=d if j == 0 else [])
                d = []
        ev = p.emit("dv", lambda E, pp=pp, ps=ps: E.tensor_copy(attb[:, pp, :],
                                                                ps[:, 0:H]),
                    deps=[mm_last, att_sts[it - 2] if it >= 2 else None])
        ev_prev[pp] = ev
        att_sts.append(p.dma("ac", att_s[128 * it:128 * (it + 1), :], attb[:, pp, :],
                             deps=[ev], tag="att_st"))

    if stop_after == "B3":
        return _stop(p, nc, stack, att_sts)
    abf.reset(); af32.reset()
    abf.reset()
    abf.mark(); af32.mark()
    xqT = abf.alloc([c.NH, CT])
    gstr = abf.alloc([2, c.NH, c.ICH])
    ustr = abf.alloc([2, c.NH, c.ICH])
    hqb = abf.alloc([2, c.ICH])
    hsil = af32.alloc([c.ICH])
    hful = af32.alloc([c.ICH])
    hma = af32.alloc([c.ICH // 32])
    hs2 = af32.alloc([c.ICH // 32])
    hrs = af32.alloc([c.ICH // 32])
    hq8 = af8.alloc([c.ICH])

    c1_guard = ([b2[s]["gh"] for s in range(c.NH)] + att_sts
                + [ev_prev[0], ev_prev[1], wout_toks[-1], mm_last])
    xqT_toks = []
    for h in range(c.NH):
        xqT_toks.append(p.dma("ac", xqT[:, h, :], xnq_s[:, 128 * h:128 * (h + 1)],
                              deps=(a_done + c1_guard), transpose=True, tag="xqT"))

    GSW = min(c.ICH, ISH)
    def load_ffn_chunk(ch, bi, deps):
        toks = []
        for h in range(c.NH):
            for s0 in range(0, c.ICH, GSW):
                r0 = c.ICH * ch + s0
                rk, lc = r0 // ISH, r0 % ISH
                toks.append(p.dma("sp", gstr[:, bi, h, s0:s0 + GSW],
                                  gwT_view(rk, OFF_G)[128 * h:128 * (h + 1),
                                                      lc:lc + GSW],
                                  deps=deps, tag="gstr"))
                toks.append(p.dma("sp", ustr[:, bi, h, s0:s0 + GSW],
                                  gwT_view(rk, OFF_U)[128 * h:128 * (h + 1),
                                                      lc:lc + GSW],
                                  deps=deps, tag="ustr"))
        return toks

    ch_toks = {0: load_ffn_chunk(0, 0, wq_done + c1_guard)}
    if c.NICH > 1:
        ch_toks[1] = load_ffn_chunk(1, 1, wq_done + c1_guard)
    hq_sts = []
    ps_free = {0: ev_prev[0], 1: ev_prev[1]}
    prev_st = None
    prev_hm = None
    prev_q8 = None
    for ch in range(c.NICH):
        bi = ch % 2
        last_mms = []
        for it in range(c.NTT):
            pp = it % 2
            ps = psA if pp == 0 else psB
            mm_last = None
            d0 = []
            if it == 0:
                d0 = ch_toks[ch] + xqT_toks
            for h in range(c.NH):
                d = list(d0) if h == 0 else []
                if h == 0 and ps_free[pp] is not None:
                    d.append(ps_free[pp])
                mm_last = p.emit("pe", (lambda E, h=h, it=it, bi=bi, ps=ps: E.matmul(
                    ps[:, 0:c.ICH], xqT[:, h, 128 * it:128 * (it + 1)],
                    gstr[:, bi, h, :], start=(h == 0), stop=(h == c.NH - 1))), deps=d)
                mm_last = p.emit("pe", (lambda E, h=h, it=it, bi=bi, ps=ps: E.matmul(
                    ps[:, c.ICH:2 * c.ICH], xqT[:, h, 128 * it:128 * (it + 1)],
                    ustr[:, bi, h, :], start=(h == 0), stop=(h == c.NH - 1))), deps=[])
            last_mms.append(mm_last)
            sil = p.emit("ac", lambda E, ps=ps: E.activation(hsil, ps[:, 0:c.ICH],
                                                             AF.Silu),
                         deps=[mm_last, prev_hm], tag="silu")
            hm = p.emit("dv", lambda E, ps=ps: E.tensor_tensor(
                hful, hsil, ps[:, c.ICH:2 * c.ICH], ALU.mult),
                deps=[sil, prev_q8])
            prev_hm = hm
            ps_free[pp] = hm
            dq, q8t = quant_tile(hful, hq8, hma, hs2, hrs,
                                 hqb[:, it % 2, :], [hm, prev_st],
                                 dq_eng="gp")
            prev_q8 = q8t
            st = p.dma("ac", hq_s[128 * it:128 * (it + 1), c.ICH * ch:c.ICH * (ch + 1)],
                       hqb[:, it % 2, :], deps=[dq, prev_st], tag="hq_st")
            prev_st = st
            hq_sts.append(st)
        nxt = ch + 2
        if nxt < c.NICH:
            ch_toks[nxt] = load_ffn_chunk(nxt, nxt % 2, last_mms)

    if stop_after == "C1":
        return _stop(p, nc, stack, hq_sts)
    abf.reset(); af32.reset()
    abf.mark(); af32.mark()
    NHALF = 2 if H > 1024 else 1
    HW2 = H // NHALF
    NW512 = HW2 // 512
    hqT = abf.alloc([2, c.NIS, CT])
    dwT = abf.alloc([2, c.NIS, HW2])
    facc = abf.alloc([c.NTT, HW2])
    xres = af32.alloc([2, HW2])
    attr = af32.alloc([2, HW2])
    fout = af32.alloc([2, HW2])

    c2_guard = hq_sts + [t for t in xqT_toks]

    def load_sec(hh, sec, bi, deps):
        toks = []
        for ii in range(c.NIS):
            itile = sec * c.NIS + ii
            toks.append(p.dma("sp", hqT[:, bi, ii, :],
                              hq_s[:, 128 * itile:128 * (itile + 1)],
                              deps=deps, transpose=True, tag="hqT"))
            rk, lr = (128 * itile) // DSH, (128 * itile) % DSH
            toks.append(p.dma("ac", dwT[:, bi, ii, :],
                              dwT_view(rk)[lr:lr + 128, HW2 * hh:HW2 * (hh + 1)],
                              deps=deps, tag="dwT"))
        return toks

    hsecs = [(hh, sec) for hh in range(NHALF) for sec in range(c.NSEC)]
    sec_toks = {0: load_sec(hsecs[0][0], hsecs[0][1], 0, wq_done + c2_guard)}
    if len(hsecs) > 1:
        sec_toks[1] = load_sec(hsecs[1][0], hsecs[1][1], 1, wq_done + c2_guard)
    out_sts = []
    ev_prev = {0: None, 1: None}
    prev_out = {0: None, 1: None}
    for g, (hh, sec) in enumerate(hsecs):
        bi = g % 2
        if sec == 0:
            acc_tok = {}
        sec_last = []
        for it in range(c.NTT):
            pp = it % 2
            ps = psA if pp == 0 else psB
            mm_last = None
            for ii in range(c.NIS):
                d = []
                if ii == 0:
                    d = sec_toks[g] + [ev_prev[pp]]
                    if g == 0 and it < 2:
                        d.append(ps_free[pp])
                for j in range(NW512):
                    mm_last = p.emit("pe", (lambda E, ii=ii, j=j, it=it, bi=bi, ps=ps:
                                            E.matmul(
                        ps[:, 512 * j:512 * (j + 1)],
                        hqT[:, bi, ii, 128 * it:128 * (it + 1)],
                        dwT[:, bi, ii, 512 * j:512 * (j + 1)],
                        start=(ii == 0), stop=(ii == c.NIS - 1))),
                        deps=d if j == 0 else [])
                    d = []
            sec_last.append(mm_last)
            is_first, is_last = sec == 0, sec == c.NSEC - 1
            if not is_last:
                if is_first:
                    ev = p.emit("dv", lambda E, it=it, ps=ps: E.tensor_copy(
                        facc[:, it, :], ps[:, 0:HW2]), deps=[mm_last], tag="dacc0")
                else:
                    ev = p.emit("dv", lambda E, it=it, ps=ps: E.tensor_tensor(
                        facc[:, it, :], facc[:, it, :], ps[:, 0:HW2], ALU.add),
                        deps=[mm_last, acc_tok[it]], tag="dacc")
                acc_tok[it] = ev
                ev_prev[pp] = ev
            else:
                ldx = p.dma("ac", xres[:, pp, :],
                            x_loc[128 * it:128 * (it + 1), HW2 * hh:HW2 * (hh + 1)],
                            deps=[prev_out[pp]], tag="xres")
                lda = p.dma("ac", attr[:, pp, :],
                            att_s[128 * it:128 * (it + 1), HW2 * hh:HW2 * (hh + 1)],
                            deps=[att_sts[it], prev_out[pp]], tag="attr")
                if is_first:
                    e1 = p.emit("dv", lambda E, it=it, ps=ps, pp=pp: E.tensor_copy(
                        fout[:, pp, :], ps[:, 0:HW2]),
                        deps=[mm_last, prev_out[pp]], tag="fin1")
                else:
                    e1 = p.emit("dv", lambda E, it=it, ps=ps, pp=pp: E.tensor_tensor(
                        fout[:, pp, :], facc[:, it, :], ps[:, 0:HW2], ALU.add),
                        deps=[mm_last, acc_tok[it], prev_out[pp]], tag="fin1")
                e2 = p.emit("dv", lambda E, pp=pp: E.tensor_tensor(
                    fout[:, pp, :], fout[:, pp, :], attr[:, pp, :], ALU.add),
                    deps=[e1, lda], tag="fin2")
                e3 = p.emit("dv", lambda E, pp=pp: E.tensor_tensor(
                    fout[:, pp, :], fout[:, pp, :], xres[:, pp, :], ALU.add),
                    deps=[e2, ldx], tag="fin3")
                ev_prev[pp] = e3
                st = p.dma("ac",
                           out_d[128 * it:128 * (it + 1), HW2 * hh:HW2 * (hh + 1)],
                           fout[:, pp, :], deps=[e3], tag="out_st")
                prev_out[pp] = st
                out_sts.append(st)
        if g + 2 < len(hsecs):
            sec_toks[g + 2] = load_sec(hsecs[g + 2][0], hsecs[g + 2][1],
                                       g % 2, sec_last)

    if debug_dumps:
        dsem_toks = []
        for nm, t in [("d_xnbf", xnbf_s), ("d_xnq", xnq_s), ("d_att", att_s),
                      ("d_hq", hq_s), ("d_anx", anx_s), ("d_cc", cc_out),
                      ("d_wsh", wsh_out)]:
            od = nc.dram_tensor(nm, list(t.shape), t.dtype, kind="ExternalOutput").ap()
            dsem_toks.append(p.dma("gp", od, t, deps=out_sts, tag="dbg"))
        out_sts = out_sts + dsem_toks
    p.emit("gp", lambda E: E.engine_nop(), deps=out_sts, tag="fence")

    with nc.Block() as blk:
        p.finalize(nc, blk, stack)
    stack.close()
    return nc


def _stop(p, nc, stack, toks):
    p.emit("gp", lambda E: E.engine_nop(), deps=toks, tag="fence")
    with nc.Block() as blk:
        p.finalize(nc, blk, stack)
    stack.close()
    return nc



def make_inputs(cfg, inp):
    c = cfg
    x = np.ascontiguousarray(np.asarray(inp["L0_attn_x"], np.float32))
    W_xy = np.asarray(inp["L0_attn_W_xy"], np.float32)
    wi = np.asarray(inp["L0_attn_wi"], np.float32)
    bi = np.asarray(inp["L0_attn_bi"], np.float32)
    wa = np.asarray(inp["L0_attn_wa"], np.float32)
    ba = np.asarray(inp["L0_attn_ba"], np.float32)
    a_param = np.asarray(inp["L0_attn_a_param"], np.float32)
    conv_w = np.asarray(inp["L0_attn_conv_w"], np.float32)
    conv_b = np.asarray(inp["L0_attn_conv_b"], np.float32)
    W_out = np.asarray(inp["L0_attn_W_out"], np.float32)
    gw = np.asarray(inp["L0_ffn_gate_w"], np.float32)
    uw = np.asarray(inp["L0_ffn_up_w"], np.float32)
    dw = np.asarray(inp["L0_ffn_down_w"], np.float32)
    ISH = gw.shape[0] // NCORES
    DSH = dw.shape[1] // NCORES

    wxyT = np.ascontiguousarray(W_xy.T)
    woutT = np.ascontiguousarray(W_out.T)
    wi_r = np.ascontiguousarray(wi.transpose(1, 0, 2))
    wa_r = np.ascontiguousarray(wa.transpose(1, 0, 2))
    bi_r = np.ascontiguousarray(bi.reshape(c.NH, 128).T)
    ba_r = np.ascontiguousarray(ba.reshape(c.NH, 128).T)
    ap_r = np.ascontiguousarray(a_param.reshape(c.NH, 128).T)
    cw_r = np.ascontiguousarray(conv_w.T.reshape(c.NH, 128, 4).transpose(1, 0, 2))
    cb_r = np.ascontiguousarray(conv_b.reshape(c.NH, 128).T)
    consts = np.tile(np.array([[CONSTS[k] for k in CONSTS]], np.float32), (128, 1))

    maps = []
    for core in range(NCORES):
        b, k = core // c.NCHUNK, core % c.NCHUNK
        t0 = k * c.CT
        xp = np.zeros((128, c.H), np.float32)
        if k > 0:
            xp[0:3] = x[b, t0 - 3:t0]
        cmask = np.zeros((128, c.NH, 6), np.float32)
        if k > 0:
            cmask[:, :, 3 * b + (k - 1)] = 1.0
        maps.append(dict(
            x_loc=np.ascontiguousarray(x[b, t0:t0 + c.CT]), x_pre=xp,
            wxyT=wxyT, woutT=woutT, wi_in=wi_r, wa_in=wa_r, bi_in=bi_r, ba_in=ba_r,
            ap_in=ap_r, cw_in=cw_r, cb_in=cb_r,
            gw_sh=np.ascontiguousarray(gw[ISH * core:ISH * (core + 1)]),
            uw_sh=np.ascontiguousarray(uw[ISH * core:ISH * (core + 1)]),
            dw_sh=np.ascontiguousarray(dw[:, DSH * core:DSH * (core + 1)]),
            ident_in=np.eye(128, dtype=ml_dtypes.bfloat16),
            consts_in=consts, cmask_in=cmask,
        ))
    return maps


def kernel(**inputs):
    cfg = FULL
    nc = build_program(cfg)
    maps = make_inputs(cfg, inputs)
    res = run_bass_kernel_spmd(nc, maps, core_ids=list(range(NCORES)))
    out = np.empty((cfg.B, cfg.T, cfg.H), np.float32)
    for core in range(NCORES):
        b, k = core // cfg.NCHUNK, core % cfg.NCHUNK
        out[b, k * cfg.CT:(k + 1) * cfg.CT] = np.asarray(res.results[core]["out"])
    return out


# version 16
# speedup vs baseline: 1.0799x; 1.0628x over previous
import numpy as np
import ml_dtypes
import contextlib
import concourse.bass as bass
import concourse.mybir as mybir
from concourse.bass_utils import run_bass_kernel_spmd

F32 = mybir.dt.float32
F16 = mybir.dt.float16
BF16 = mybir.dt.bfloat16
FP8 = mybir.dt.float8e4
AF = mybir.ActivationFunctionType
ALU = mybir.AluOpType
AX = mybir.AxisListType

NCORES = 8


class Cfg:
    def __init__(self, B=2, T=8192, H=2048, I=8192):
        self.B, self.T, self.H, self.I = B, T, H, I
        self.COLS = 2 * H
        self.NCHUNK = NCORES // B
        self.CT = T // self.NCHUNK
        assert self.CT % 512 == 0 and H % 128 == 0 and I % 512 == 0
        self.NTT = self.CT // 128
        self.NH = H // 128
        self.NI = I // 128
        self.ICH = 512
        self.NICH = I // self.ICH
        self.SEC = 1024 if I % 1024 == 0 else I
        self.NSEC = I // self.SEC
        self.NIS = self.SEC // 128
        self.CTP = self.CT + 128
        self.NT512 = self.CT // 512
        self.NH512 = H // 512


FULL = Cfg()



class Prog:
    ENGS = ("pe", "dv", "ac", "gp", "sp")

    def __init__(self):
        self.ops = []
        self.n = {e: 0 for e in self.ENGS}

    def emit(self, eng, fn, deps=(), asyn=False, amt=1, tag=""):
        tok = (eng, self.n[eng])
        self.ops.append(dict(e=eng, i=self.n[eng], fn=fn,
                             deps=[d for d in deps if d is not None],
                             asyn=asyn, amt=amt, marked=False, tag=tag))
        self.n[eng] += 1
        return tok

    def dma(self, eng, out, in_, deps=(), transpose=False, tag=""):
        if transpose:
            eng = "sp"
            fn = (lambda E, o=out, i=in_: E.dma_start(out=o, in_=i, transpose=True))
        else:
            fn = (lambda E, o=out, i=in_: E.dma_start(out=o, in_=i))
        return self.emit(eng, fn, deps, asyn=True, amt=16, tag=tag)

    NPOOL = 8

    def finalize(self, nc, blk, stack):
        byeng = {e: [] for e in self.ENGS}
        for op in self.ops:
            byeng[op["e"]].append(op)
        idx = {(op["e"], op["i"]): op for op in self.ops}
        for op in self.ops:
            for d in op["deps"]:
                pr = idx[d]
                if pr["asyn"]:
                    continue
                if pr["e"] == "pe" and op["e"] == "pe":
                    continue
                pr["marked"] = True
        sems = {}
        pool = {}
        for e in self.ENGS:
            sems[e] = stack.enter_context(nc.semaphore(f"s_{e}"))
        for e in self.ENGS:
            aops = [op for op in byeng[e] if op["asyn"] and op["amt"] == 16]
            ccops = [op for op in byeng[e] if op["asyn"] and op["amt"] != 16]
            for j, op in enumerate(ccops):
                lane = self.NPOOL + j
                key = (e, lane)
                pool[key] = stack.enter_context(nc.semaphore(f"q_{e}cc{j}"))
                op["lane"] = lane
                op["chain_wait"] = 0
                op["val"] = op["amt"]
                op["marked"] = True
            lane_cnt = {}
            for j, op in enumerate(aops):
                lane = j % self.NPOOL
                key = (e, lane)
                if key not in pool:
                    pool[key] = stack.enter_context(nc.semaphore(f"q_{e}{lane}"))
                prev = lane_cnt.get(lane, 0)
                op["lane"] = lane
                op["chain_wait"] = prev
                op["val"] = prev + op["amt"]
                lane_cnt[lane] = op["val"]
                op["marked"] = True
            v = 0
            for op in byeng[e]:
                if op["asyn"]:
                    continue
                if op["marked"]:
                    v += op["amt"]
                op["val"] = v

        def mk(e):
            def run(E):
                last = {}
                for op in byeng[e]:
                    wants = {}
                    for d in op["deps"]:
                        pr = idx[d]
                        if pr["asyn"]:
                            k = (pr["e"], pr["lane"])
                            wants[k] = max(wants.get(k, 0), pr["val"])
                        elif pr["e"] == "pe" and e == "pe":
                            continue
                        else:
                            wants[pr["e"]] = max(wants.get(pr["e"], 0), pr["val"])
                    if op.get("asyn") and op["chain_wait"] > 0:
                        k = (e, op["lane"])
                        wants[k] = max(wants.get(k, 0), op["chain_wait"])
                    for k, v in wants.items():
                        if last.get(k, 0) < v:
                            E.wait_ge(pool[k] if isinstance(k, tuple) else sems[k], v)
                            last[k] = v
                    ins = op["fn"](E)
                    if op["asyn"]:
                        ins.then_inc(pool[(e, op["lane"])], op["amt"])
                    elif op["marked"]:
                        ins.then_inc(sems[e], op["amt"])
            return run

        blk.tensor(mk("pe"))
        blk.vector(mk("dv"))
        blk.scalar(mk("ac"))
        blk.gpsimd(mk("gp"))
        blk.sync(mk("sp"))


class Arena:

    def __init__(self, nc, stack, name, nelem, dt, **kw):
        self.t = stack.enter_context(nc.sbuf_tensor(name, [128, nelem], dt, **kw))
        self.nelem = nelem
        self.off = 0
        self.marks = []

    def alloc(self, shape):
        n = 1
        for s in shape:
            n *= s
        o = (self.off + 63) // 64 * 64
        self.off = o + n
        assert self.off <= self.nelem, f"arena overflow {self.off} > {self.nelem}"
        ap = self.t.ap()[:, o:o + n]
        if len(shape) > 1:
            kw = {}
            names = "abcd"[: len(shape)]
            pat = f"p ({' '.join(names)}) -> p {' '.join(names)}"
            kw = {names[j]: shape[j] for j in range(1, len(shape))}
            ap = ap.rearrange(pat, **kw)
        return ap

    def mark(self):
        self.marks.append(self.off)

    def reset(self):
        self.off = self.marks.pop()


CONSTS = {"one": 1.0, "half": 0.5}
CIDX = {k: i for i, k in enumerate(CONSTS)}


def build_program(cfg: Cfg, stop_after=None, debug_dumps=False):
    nc = bass.Bass("TRN2", target_bir_lowering=False)
    c = cfg
    H, I, CT = c.H, c.I, c.CT
    stack = contextlib.ExitStack()
    p = Prog()

    def din(name, shape, dt=F32):
        return nc.dram_tensor(name, shape, dt, kind="ExternalInput").ap()

    x_loc = din("x_loc", [CT, H])
    x_pre = din("x_pre", [128, H])
    wxyT = din("wxyT", [H, c.COLS])
    woutT = din("woutT", [H, H])
    wi_in = din("wi_in", [128, c.NH, 128])
    wa_in = din("wa_in", [128, c.NH, 128])
    bi_in = din("bi_in", [128, c.NH])
    ba_in = din("ba_in", [128, c.NH])
    ap_in = din("ap_in", [128, c.NH])
    cw_in = din("cw_in", [128, c.NH, 4])
    cb_in = din("cb_in", [128, c.NH])
    ISH = I // NCORES
    DSH = I // NCORES
    gw_in = din("gw_sh", [ISH, H])
    uw_in = din("uw_sh", [ISH, H])
    dw_in = din("dw_sh", [H, DSH])
    ident_in = din("ident_in", [128, 128], BF16)
    consts_in = din("consts_in", [128, len(CONSTS)])
    cmask_in = din("cmask_in", [128, c.NH, 6])
    out_d = nc.dram_tensor("out", [CT, H], F32, kind="ExternalOutput").ap()

    def dscr(name, shape, dt, **kw):
        return nc.dram_tensor(name, shape, dt, kind="Internal", **kw).ap()

    xnbf_s = dscr("xnbf_s", [c.CTP, H], BF16)
    xnq_s = dscr("xnq_s", [CT, H], BF16)
    WSH = ISH * H + ISH * H + H * DSH
    OFF_G, OFF_U, OFF_D = 0, ISH * H, 2 * ISH * H
    anx_s = dscr("anx_s", [2, c.NH, 128, CT], BF16)
    hq_s = dscr("hq_s", [CT, I], BF16)
    att_s = dscr("att_s", [CT, H], F32)
    from concourse.replica_groups import maybe_share_collective_output_space
    RG = [list(range(NCORES))]
    ccsp = maybe_share_collective_output_space("AllGather", RG, prefer_local=True)
    cc_in = dscr("cc_in", [2, 128, c.NH], F32)
    cc_out = dscr("cc_out", [NCORES * 2, 128, c.NH], F32, addr_space=ccsp)
    wsh_in = dscr("wsh_in", [1, WSH], BF16)
    wsh_out = dscr("wsh_out", [NCORES, WSH], BF16, addr_space=ccsp)

    def gwT_view(rk, off):
        return wsh_out[rk:rk + 1, off:off + ISH * H].rearrange(
            "q (a b) -> (q a) b", b=ISH)

    def dwT_view(rk):
        return wsh_out[rk:rk + 1, OFF_D:OFF_D + H * DSH].rearrange(
            "q (a b) -> (q a) b", b=H)

    abf = Arena(nc, stack, "abf", 78 * 1024, BF16, is_transpose_dest=True)
    af32 = Arena(nc, stack, "af32", 10 * 1024, F32)
    af8 = Arena(nc, stack, "af8", 5 * 1024, FP8)

    csts = af32.alloc([len(CONSTS)])
    bi_sb = af32.alloc([c.NH])
    ba_sb = af32.alloc([c.NH])
    msp_sb = af32.alloc([c.NH])
    cw_sb = af32.alloc([c.NH, 4])
    cb_sb = af32.alloc([c.NH])
    cmask_sb = af32.alloc([c.NH, 6])
    tiny = af32.alloc([2, max(16, c.NH)])
    hfin = af32.alloc([c.NH])
    Ach = af32.alloc([c.NH])
    ssum = af32.alloc([c.NH])
    h_in = af32.alloc([c.NH])
    gath = af32.alloc([16, c.NH])
    cand = af32.alloc([6, c.NH])
    wi_sb = abf.alloc([c.NH, 128])
    wa_sb = abf.alloc([c.NH, 128])

    CST = lambda name: csts[:, CIDX[name]:CIDX[name] + 1]

    ld_csts = p.dma("sp", csts, consts_in, tag="csts")
    ld_wi = p.dma("gp", wi_sb, wi_in, tag="wi")
    ld_wa = p.dma("gp", wa_sb, wa_in, tag="wa")
    ld_bi = p.dma("sp", bi_sb, bi_in)
    ld_ba = p.dma("sp", ba_sb, ba_in)
    ld_ap = p.dma("sp", msp_sb, ap_in)
    ld_cw = p.dma("sp", cw_sb, cw_in)
    ld_cb = p.dma("sp", cb_sb, cb_in)
    ld_cm = p.dma("sp", cmask_sb, cmask_in)

    t_e = p.emit("ac", lambda E: E.activation(tiny[:, 0, 0:c.NH], msp_sb, AF.Exp),
                 deps=[ld_ap])
    t_l = p.emit("ac", lambda E: E.activation(tiny[:, 1, 0:c.NH], tiny[:, 0, 0:c.NH],
                                              AF.Ln, bias=CST("one")),
                 deps=[t_e, ld_csts])
    t_m = p.emit("dv", lambda E: E.tensor_scalar_mul(msp_sb, tiny[:, 1, 0:c.NH], -8.0),
                 deps=[t_l])

    psA = stack.enter_context(nc.psum_tensor("psA", [128, 2048], F32))
    psB = stack.enter_context(nc.psum_tensor("psB", [128, 2048], F32))

    def quant_tile(x_ap, q8_ap, ma, s2, rs, outbf_ap, deps, dq_eng="gp"):
        feat = x_ap.shape[-1]
        nb = feat // 32
        x3 = x_ap.rearrange("p (a b) -> p a b", b=32)
        t_ma = p.emit("dv", lambda E: E.tensor_reduce(ma[:, 0:nb], x3, AX.X, ALU.max,
                                                      apply_absolute_value=True), deps=deps)
        t_s2 = p.emit("dv", lambda E: E.tensor_scalar(s2[:, 0:nb], ma[:, 0:nb],
                                                      1.0 / 224.0, 2e-30, ALU.mult,
                                                      ALU.max), deps=[t_ma])
        t_rs = p.emit("dv", lambda E: E.reciprocal(rs[:, 0:nb], s2[:, 0:nb]), deps=[t_s2])
        q83 = q8_ap.rearrange("p (a b) -> p a b", b=32)
        t_q8 = p.emit("dv", lambda E: E.tensor_tensor(
            q83, x3, rs[:, 0:nb].unsqueeze(2).broadcast_to([128, nb, 32]), ALU.mult),
            deps=[t_rs])
        o3 = outbf_ap.rearrange("p (a b) -> p a b", b=32)
        t_dq = p.emit(dq_eng, lambda E: E.tensor_tensor(
            o3, q83, s2[:, 0:nb].unsqueeze(2).broadcast_to([128, nb, 32]), ALU.mult),
            deps=[t_q8])
        return t_dq, t_q8

    abf.mark()
    xnT = abf.alloc([c.NH, CT])
    wxy_pan = abf.alloc([2, c.NH, 128])
    xnT_cols = {}

    af32.mark(); abf.mark(); af8.mark()
    WF = min(2048, max(H, 2048))
    NWB = 2
    wq_x = af32.alloc([NWB, WF])
    wq_bf = abf.alloc([NWB, WF])
    wq_q8 = af8.alloc([1, WF])
    wq_ma = af32.alloc([NWB, WF // 32])
    wq_s2 = af32.alloc([NWB, WF // 32])
    wq_rs = af32.alloc([NWB, WF // 32])

    wqT_bf = abf.alloc([NWB, WF // 128, 128])
    ident = abf.alloc([128])
    ld_id = p.dma("gp", ident, ident_in, tag="ident")

    _wst = dict(terms={}, trp={}, q8prev={}, n=0)

    def emit_w_tile(srcw, dstT, r, feat, deps0=()):
        i = _wst["n"]; _wst["n"] += 1
        bi = i % NWB
        qi = 0
        ld = p.dma("ac", wq_x[:, bi, 0:feat], srcw[128 * r:128 * (r + 1), :],
                   deps=list(deps0) + [_wst["terms"].get(bi)], tag="wq_ld")
        dq, q8t = quant_tile(wq_x[:, bi, 0:feat], wq_q8[:, qi, 0:feat],
                             wq_ma[:, bi, :], wq_s2[:, bi, :], wq_rs[:, bi, :],
                             wq_bf[:, bi, 0:feat],
                             [ld, _wst["q8prev"].get(qi)], dq_eng="gp")
        _wst["q8prev"][qi] = dq
        evs = []
        for j in range(feat // 128):
            sl = (j % 4) * 512
            tr = p.emit("pe", (lambda E, bi=bi, j=j, sl=sl: E.transpose(
                psA[:, sl:sl + 64].bitcast(BF16),
                wq_bf[:, bi, 128 * j:128 * (j + 1)], ident)),
                deps=[dq, ld_id, _wst["trp"].get(j % 4)], tag="wtr")
            ev = p.emit("dv", (lambda E, bi=bi, j=j, sl=sl: E.tensor_copy(
                wqT_bf[:, bi, j, :], psA[:, sl:sl + 64].bitcast(BF16))),
                deps=[tr, _wst["terms"].get(bi)], tag="wtr_ev")
            _wst["trp"][j % 4] = ev
            evs.append(ev)
        st = p.dma("ac",
                   dstT[:, 128 * r:128 * (r + 1)].rearrange(
                       "(j q) c -> q j c", q=128),
                   wqT_bf[:, bi, 0:feat // 128, :], deps=evs, tag="wq_st")
        _wst["terms"][bi] = st
        return st

    gq_view = wsh_in[0:1, OFF_G:OFF_G + ISH * H].rearrange("q (a b) -> (q a) b", b=ISH)
    uq_view = wsh_in[0:1, OFF_U:OFF_U + ISH * H].rearrange("q (a b) -> (q a) b", b=ISH)
    dq_view = wsh_in[0:1, OFF_D:OFF_D + H * DSH].rearrange("q (a b) -> (q a) b", b=H)
    w_tiles = ([(gw_in, gq_view, r, H) for r in range(ISH // 128)]
               + [(uw_in, uq_view, r, H) for r in range(ISH // 128)]
               + [(dw_in, dq_view, r, DSH) for r in range(H // 128)])
    NAB = 2
    ax = abf.alloc([NAB, 2 * H]).bitcast(F32)
    axn = abf.alloc([NAB, 2 * H]).bitcast(F32)
    axnb = abf.alloc([NAB, H])
    axqb = abf.alloc([NAB, H])
    ast = af32.alloc([NAB, 8])
    ama = af32.alloc([NAB, H // 32])
    as2 = af32.alloc([NAB, H // 32])
    ars = af32.alloc([NAB, H // 32])
    axq8 = af8.alloc([1, H])
    _ast = dict(q8prev={}, n=0)

    def phaseA_tile(src_rows, dst_bf, dst_q, bi, deps_in):
        ld = p.dma("ac", ax[:, bi, :], src_rows, deps=deps_in, tag="A_ld")
        s1 = p.emit("dv", lambda E: E.reduce_sum(ast[:, bi, 0:1], ax[:, bi, :], AX.X),
                    deps=[ld])
        ssq = p.emit("ac", lambda E: E.activation(axn[:, bi, :], ax[:, bi, :],
                                                  AF.Square,
                                                  accum_out=ast[:, bi, 2:3]),
                     deps=[ld], tag="ssq")
        m = p.emit("dv", lambda E: E.tensor_scalar_mul(ast[:, bi, 1:2], ast[:, bi, 0:1],
                                                       1.0 / H), deps=[s1])
        m2 = p.emit("dv", lambda E: E.tensor_tensor(ast[:, bi, 3:4], ast[:, bi, 1:2],
                                                    ast[:, bi, 1:2], ALU.mult), deps=[m])
        m2e = p.emit("dv", lambda E: E.tensor_scalar(ast[:, bi, 4:5], ast[:, bi, 3:4],
                                                     1.0, 1e-5, ALU.mult, ALU.subtract),
                     deps=[m2])
        var = p.emit("dv", lambda E: E.tensor_scalar(ast[:, bi, 5:6], ast[:, bi, 2:3],
                                                     1.0 / H, ast[:, bi, 4:5], ALU.mult,
                                                     ALU.subtract), deps=[ssq, m2e])
        r1 = p.emit("dv", lambda E: E.reciprocal(ast[:, bi, 6:7], ast[:, bi, 5:6]),
                    deps=[var])
        rstd = p.emit("ac", lambda E: E.activation(ast[:, bi, 7:8], ast[:, bi, 6:7],
                                                   AF.Sqrt), deps=[r1])
        xn = p.emit("dv", lambda E: E.tensor_scalar(axn[:, bi, :], ax[:, bi, :],
                                                    ast[:, bi, 1:2], ast[:, bi, 7:8],
                                                    ALU.subtract, ALU.mult),
                    deps=[rstd, ssq], tag="xn")
        xnb = p.emit("ac", lambda E: E.activation(axnb[:, bi, :], axn[:, bi, :],
                                                  AF.Copy), deps=[xn], tag="xnb")
        stb = p.dma("gp", dst_bf, axnb[:, bi, :], deps=[xnb], tag="A_stb")
        done = [stb]
        if dst_q is not None:
            qi = 0
            _ast["n"] += 1
            dq, _ = quant_tile(axn[:, bi, :], axq8[:, qi, :], ama[:, bi, :],
                               as2[:, bi, :], ars[:, bi, :],
                               axqb[:, bi, :],
                               [xn, _ast["q8prev"].get(qi)], dq_eng="gp")
            _ast["q8prev"][qi] = dq
            stq = p.dma("gp", dst_q, axqb[:, bi, :], deps=[dq], tag="A_stq")
            done.append(stq)
        return done

    a_done_tiles = []
    wq_sts = []
    a_idx = 0
    nW, nA = len(w_tiles), c.NTT + 1

    def emit_a(i):
        bi = i % NAB
        deps_in = a_done_tiles[i - NAB] if i >= NAB else []
        if i < c.NTT:
            a_done_tiles.append(phaseA_tile(
                x_loc[128 * i:128 * (i + 1), :],
                xnbf_s[128 * i:128 * (i + 1), :],
                xnq_s[128 * i:128 * (i + 1), :], bi, deps_in))
        else:
            a_done_tiles.append(phaseA_tile(x_pre, xnbf_s[CT:CT + 128, :], None,
                                            bi, deps_in))
        if i < c.NTT and i % 4 == 3:
            j = i // 4
            stbs = [a_done_tiles[t][0] for t in range(4 * j, 4 * j + 4)]
            for h in range(c.NH):
                xnT_cols[(h, j)] = p.dma(
                    "sp", xnT[:, h, 512 * j:512 * (j + 1)],
                    xnbf_s[512 * j:512 * (j + 1), 128 * h:128 * (h + 1)],
                    deps=stbs + wq_sts[-2:], transpose=True, tag="xnT")

    for wi_, (srcw, dstT, r, feat) in enumerate(w_tiles):
        wq_sts.append(emit_w_tile(srcw, dstT, r, feat,
                                  deps0=[t_m] if wi_ == 0 else []))
        while a_idx < nA and a_idx * nW <= (wi_ + 1) * nA:
            emit_a(a_idx)
            a_idx += 1
    while a_idx < nA:
        emit_a(a_idx)
        a_idx += 1
    a_done = [t for tile in a_done_tiles for t in tile]

    ccw = p.emit("gp", lambda E: E.collective_compute(
        "AllGather", ALU.bypass, RG, [wsh_in.flatten()], [wsh_out.flatten()]),
        deps=wq_sts, asyn=True, amt=1, tag="allgather_w")
    wq_done = [ccw]
    af32.reset(); abf.reset(); af8.reset()
    if stop_after == "A":
        return _stop(p, nc, stack, a_done)


    abf.mark(); af32.mark()
    xnTp = abf.alloc([c.NH, 128])
    xnTp_toks = []
    for h in range(c.NH):
        xnTp_toks.append(p.dma("sp", xnTp[:, h, :],
                              xnbf_s[CT:c.CTP, 128 * h:128 * (h + 1)],
                              deps=a_done, transpose=True, tag="xnTp"))
    xp = abf.alloc([CT + 4])
    cacc = abf.alloc([2, CT])
    xr2 = abf.alloc([2, CT])
    gx2 = abf.alloc([2, CT])
    ga2 = abf.alloc([2, CT])
    ab2 = abf.alloc([2, CT])
    nx2 = abf.alloc([2, CT])
    loga = af32.alloc([2, CT])
    sscr = af32.alloc([CT])
    mscr = af32.alloc([CT])

    def load_wxy_panel(colt, bi, deps):
        src = wxyT[:, 128 * colt:128 * (colt + 1)].rearrange("(a q) e -> q a e", q=128)
        return p.dma("gp", wxy_pan[:, bi, :, :], src, deps=deps, tag="wxy")

    def b1_stage1(s, pan_tok, gi, prev):
        bi = s % 2
        mm_last = None
        for h in range(c.NH):
            d = []
            if h == 0:
                d = [pan_tok, prev.get("psA"), prev.get("slot1")]
            for j in range(c.NT512):
                mm_last = p.emit("pe", (lambda E, h=h, j=j, bi=bi: E.matmul(
                    psA[:, 512 * j:512 * (j + 1)], wxy_pan[:, bi, h, :],
                    xnT[:, h, 512 * j:512 * (j + 1)],
                    start=(h == 0), stop=(h == c.NH - 1))),
                    deps=d + [xnT_cols[(h, j)]])
                d = []
            mm_last = p.emit("pe", (lambda E, h=h, bi=bi: E.matmul(
                psB[:, 0:3], wxy_pan[:, bi, h, :], xnTp[:, h, 0:3],
                start=(h == 0), stop=(h == c.NH - 1))),
                deps=[xnTp_toks[h]] + ([prev.get("psB")] if h == 0 else []))
        evp = p.emit("dv", lambda E: E.tensor_copy(xp[:, 0:3], psB[:, 0:3]),
                     deps=[mm_last, prev.get("xp")])
        evx = p.emit("dv", lambda E: E.tensor_copy(xp[:, 3:3 + CT], psA[:, 0:CT]),
                     deps=[mm_last])
        c0 = p.emit("dv", lambda E, s=s: E.tensor_scalar(
            cacc[:, 0, :], xp[:, 0:CT], cw_sb[:, s, 0:1], cb_sb[:, s:s + 1],
            ALU.mult, ALU.add), deps=[evp, evx, ld_cw, ld_cb, prev.get("cacc")])
        c1 = p.emit("dv", lambda E, s=s: E.scalar_tensor_tensor(
            cacc[:, 1, :], xp[:, 1:1 + CT], cw_sb[:, s, 1:2], cacc[:, 0, :],
            ALU.mult, ALU.add), deps=[c0])
        c2 = p.emit("dv", lambda E, s=s: E.scalar_tensor_tensor(
            cacc[:, 0, :], xp[:, 2:2 + CT], cw_sb[:, s, 2:3], cacc[:, 1, :],
            ALU.mult, ALU.add), deps=[c1])
        c3 = p.emit("dv", lambda E, s=s, gi=gi: E.scalar_tensor_tensor(
            xr2[:, gi, :], xp[:, 3:3 + CT], cw_sb[:, s, 3:4], cacc[:, 0, :],
            ALU.mult, ALU.add), deps=[c2, prev.get("slot2")])
        gmm = []
        for j in range(c.NT512):
            m1 = p.emit("pe", (lambda E, j=j, s=s, gi=gi: E.matmul(
                psA[:, 512 * j:512 * (j + 1)], wi_sb[:, s, :],
                xr2[:, gi, 512 * j:512 * (j + 1)], start=True, stop=True)),
                deps=[c3, evx, ld_wi] if j == 0 else [])
            m2 = p.emit("pe", (lambda E, j=j, s=s, gi=gi: E.matmul(
                psB[:, 512 * j:512 * (j + 1)], wa_sb[:, s, :],
                xr2[:, gi, 512 * j:512 * (j + 1)], start=True, stop=True)),
                deps=[evp, ld_wa] if j == 0 else [])
            gmm += [m1, m2]
        sx = p.emit("ac", lambda E, s=s, gi=gi: E.activation(
            gx2[:, gi, :], psA[:, 0:CT], AF.Sigmoid, bias=bi_sb[:, s:s + 1]),
            deps=gmm + [ld_bi, prev.get("slot3")], tag="sigmoid")
        sa = p.emit("ac", lambda E, s=s, gi=gi: E.activation(
            ga2[:, gi, :], psB[:, 0:CT], AF.Sigmoid, bias=ba_sb[:, s:s + 1]),
            deps=gmm + [ld_ba, prev.get("slot4")], tag="sigmoid")
        return dict(sx=sx, sa=sa, xr=c3, xp_last=c3, mm_last=mm_last,
                    cacc_last=c3, evp=evp)

    def b1_stage2(s, gi, st1, prev2):
        la = p.emit("dv", lambda E, gi=gi, s=s: E.tensor_scalar_mul(
            loga[:, gi, :], ga2[:, gi, :], msp_sb[:, s:s + 1]), deps=[st1["sa"], t_m])
        a_b = p.emit("ac", lambda E, gi=gi: E.activation(ab2[:, gi, :], loga[:, gi, :],
                                                         AF.Exp),
                     deps=[la, prev2.get("a_st"), prev2.get("scan")], tag="lnexp")
        aq = p.emit("dv", lambda E, gi=gi: E.tensor_tensor(sscr[:], ab2[:, gi, :],
                                                           ab2[:, gi, :], ALU.mult),
                    deps=[a_b, prev2.get("nx")])
        m1 = p.emit("dv", lambda E: E.tensor_scalar(mscr[:], sscr[:], -1.0, 1.0,
                                                    ALU.mult, ALU.add),
                    deps=[aq, prev2.get("hf")])
        ln = p.emit("ac", lambda E: E.activation(sscr[:], mscr[:], AF.Ln),
                    deps=[m1], tag="lnexp")
        ex = p.emit("ac", lambda E: E.activation(mscr[:], sscr[:], AF.Exp,
                                                 scale=CST("half")),
                    deps=[ln, ld_csts], tag="lnexp")
        n1 = p.emit("dv", lambda E, gi=gi: E.tensor_tensor(sscr[:], xr2[:, gi, :],
                                                           gx2[:, gi, :], ALU.mult),
                    deps=[st1["xr"], st1["sx"], ex])
        nx = p.emit("dv", lambda E, gi=gi: E.tensor_tensor(nx2[:, gi, :], sscr[:],
                                                           mscr[:], ALU.mult),
                    deps=[n1, prev2.get("n_st"), prev2.get("scan")])
        asum = p.emit("dv", lambda E, gi=gi, s=s: E.reduce_sum(ssum[:, s:s + 1],
                                                               loga[:, gi, :], AX.X),
                      deps=[la])
        aex = p.emit("ac", lambda E, s=s: E.activation(Ach[:, s:s + 1], ssum[:, s:s + 1],
                                                       AF.Exp), deps=[asum], tag="lnexp")
        sc = p.emit("dv", lambda E, gi=gi: E.tensor_tensor_scan(
            mscr[:], ab2[:, gi, :], nx2[:, gi, :], 0.0, ALU.mult, ALU.add),
            deps=[nx], tag="scan1")
        hf = p.emit("dv", lambda E, s=s: E.tensor_copy(hfin[:, s:s + 1],
                                                       mscr[:, CT - 1:CT]), deps=[sc])
        sta = p.dma("gp", anx_s[0, s], ab2[:, gi, :], deps=[a_b], tag="anx_a")
        stn = p.dma("gp", anx_s[1, s], nx2[:, gi, :], deps=[nx], tag="anx_n")
        return dict(a_st=sta, n_st=stn, hf=hf, aex=aex, la=la, nx=nx, scan=sc)

    GROUP = 2
    pan_toks = {}
    pan_free = {0: None, 1: None}
    for s0 in range(min(2, c.NH)):
        pan_toks[s0] = load_wxy_panel(s0, s0 % 2, [])
    st1_prev = {"sx": wq_sts[-1], "sa": wq_sts[-1]}
    st2_prev = {0: {}, 1: {}}
    strip2 = {}
    s_list = list(range(c.NH))
    for g0 in range(0, c.NH, GROUP):
        grp = s_list[g0:g0 + GROUP]
        st1s = {}
        for gi, s in enumerate(grp):
            prev = dict(
                psA=st1_prev.get("sx"), psB=st1_prev.get("sa"),
                xp=st1_prev.get("xp_last"), cacc=st1_prev.get("cacc_last"),
                slot1=st2_prev[gi].get("la"),
                slot2=st2_prev[gi].get("nx"),
                slot3=st2_prev[gi].get("nx"),
                slot4=st2_prev[gi].get("la"),
            )
            st1s[gi] = b1_stage1(s, pan_toks[s], gi, prev)
            pan_free[s % 2] = st1s[gi]["mm_last"]
            st1_prev = st1s[gi]
            if (s + 2) < c.NH and (s + 2) not in pan_toks:
                pan_toks[s + 2] = load_wxy_panel(s + 2, (s + 2) % 2,
                                                 [pan_free[(s + 2) % 2]])
        for gi, s in enumerate(grp):
            st2 = b1_stage2(s, gi, st1s[gi], st2_prev[gi])
            st2_prev[gi] = st2
            strip2[s] = st2

    if stop_after == "B1":
        return _stop(p, nc, stack,
                     [strip2[s]["hf"] for s in range(c.NH)]
                     + [strip2[s]["a_st"] for s in range(c.NH)]
                     + [strip2[s]["n_st"] for s in range(c.NH)]
                     + [strip2[s]["aex"] for s in range(c.NH)])
    hA_st1 = p.dma("ac", cc_in[0], hfin, deps=[strip2[s]["hf"] for s in range(c.NH)],
                   tag="cc_st")
    hA_st2 = p.dma("ac", cc_in[1], Ach, deps=[strip2[s]["aex"] for s in range(c.NH)],
                   tag="cc_st")
    cc = p.emit("gp", lambda E: E.collective_compute(
        "AllGather", ALU.bypass, RG, [cc_in.flatten()], [cc_out.flatten()]),
        deps=[hA_st1, hA_st2], asyn=True, amt=1, tag="allgather")
    ld_g = p.dma("ac", gath, cc_out.rearrange("r p n -> p r n"), deps=[cc], tag="cc_ld")
    cand_toks = []
    for b in range(c.B):
        rb = b * c.NCHUNK
        t0 = p.emit("dv", lambda E, b=b, rb=rb: E.tensor_copy(
            cand[:, 3 * b + 0, :], gath[:, 2 * rb + 0, :]), deps=[ld_g])
        prevt = t0
        for kk in range(1, 3):
            r = rb + kk
            tm = p.emit("dv", lambda E, b=b, kk=kk, r=r: E.tensor_tensor(
                tiny[:, 0, 0:c.NH], gath[:, 2 * r + 1, :],
                cand[:, 3 * b + kk - 1, :], ALU.mult), deps=[prevt])
            prevt = p.emit("dv", lambda E, b=b, kk=kk, r=r: E.tensor_tensor(
                cand[:, 3 * b + kk, :], tiny[:, 0, 0:c.NH], gath[:, 2 * r + 0, :],
                ALU.add), deps=[tm])
        cand_toks.append(prevt)
    hz = p.emit("dv", lambda E: E.memset(h_in, 0.0), deps=[])
    prevt = hz
    for j in range(6):
        tm = p.emit("dv", lambda E, j=j: E.tensor_tensor(
            tiny[:, 1, 0:c.NH], cand[:, j, :], cmask_sb[:, :, j], ALU.mult),
            deps=cand_toks + [ld_cm, prevt])
        prevt = p.emit("dv", lambda E: E.tensor_tensor(
            h_in, h_in, tiny[:, 1, 0:c.NH], ALU.add), deps=[tm])
    hin_tok = prevt

    if stop_after == "CC":
        return _stop(p, nc, stack, [hin_tok])
    abf.reset(); af32.reset()
    abf.mark(); af32.mark()
    a_rb = af32.alloc([CT]).bitcast(BF16).rearrange("p (a b) -> p a b", a=2)
    n_rb = af32.alloc([CT]).bitcast(BF16).rearrange("p (a b) -> p a b", a=2)
    gyh = abf.alloc([c.NH, CT])
    hstr = af32.alloc([CT])

    b2_guard = ([strip2[s]["a_st"] for s in range(c.NH)]
                + [strip2[s]["n_st"] for s in range(c.NH)]
                + [strip2[c.NH - 1]["nx"], strip2[max(0, c.NH - 2)]["nx"]])
    prev_gy = None
    for s in range(c.NH):
        bi = s % 2
        ypan = load_wxy_panel(c.NH + s, bi, [pan_free[bi]])
        mm_last = None
        for h in range(c.NH):
            d = []
            if h == 0:
                d = [ypan, prev_gy]
                if s == 0:
                    d += [st1_prev["sx"]] + b2_guard
            for j in range(c.NT512):
                mm_last = p.emit("pe", (lambda E, h=h, j=j, bi=bi: E.matmul(
                    psA[:, 512 * j:512 * (j + 1)], wxy_pan[:, bi, h, :],
                    xnT[:, h, 512 * j:512 * (j + 1)],
                    start=(h == 0), stop=(h == c.NH - 1))), deps=d if j == 0 else [])
                d = []
        pan_free[bi] = mm_last
        gy = p.emit("ac", lambda E, s=s: E.activation(gyh[:, s, :], psA[:, 0:CT],
                                                      AF.Gelu),
                    deps=[mm_last, prev_gy], tag="gelu")
        prev_gy = gy
    last_gy = prev_gy
    b2 = {}
    prev_gh = None
    rb_toks = {}
    for s in range(min(2, c.NH)):
        rb_toks[s] = (p.dma("ac", a_rb[:, s % 2, :], anx_s[0, s],
                            deps=b2_guard + [strip2[s]["a_st"]], tag="a_rb"),
                      p.dma("ac", n_rb[:, s % 2, :], anx_s[1, s],
                            deps=[strip2[s]["n_st"]], tag="n_rb"))
    for s in range(c.NH):
        bi = s % 2
        at, ntk = rb_toks[s]
        sc = p.emit("dv", lambda E, s=s, bi=bi: E.tensor_tensor_scan(
            hstr, a_rb[:, bi, :], n_rb[:, bi, :], h_in[:, s:s + 1],
            ALU.mult, ALU.add),
            deps=[at, ntk, hin_tok, prev_gh], tag="scan2")
        gh = p.emit("dv", lambda E, s=s: E.tensor_tensor(gyh[:, s, :], gyh[:, s, :],
                                                         hstr, ALU.mult),
                    deps=[sc, last_gy], tag="gh")
        prev_gh = gh
        b2[s] = dict(gh=gh)
        if s + 2 < c.NH:
            rb_toks[s + 2] = (p.dma("ac", a_rb[:, bi, :], anx_s[0, s + 2],
                                    deps=[sc, strip2[s + 2]["a_st"]], tag="a_rb"),
                              p.dma("ac", n_rb[:, bi, :], anx_s[1, s + 2],
                                    deps=[sc, strip2[s + 2]["n_st"]], tag="n_rb"))

    if stop_after == "B2":
        return _stop(p, nc, stack, [b2[s]["gh"] for s in range(c.NH)])
    abf2_guard = [b2[s]["gh"] for s in range(c.NH)] + [pan_free[0], pan_free[1]]
    woutb = xnT
    af32.reset(); af32.mark()
    attb = af32.alloc([2, H])
    wout_toks = []
    for r in range(c.NH):
        wout_toks.append(p.dma("gp", woutb[:, r, 0:H], woutT[128 * r:128 * (r + 1), :],
                               deps=abf2_guard, tag="wout"))
    att_sts = []
    ev_prev = {0: None, 1: None}
    for it in range(c.NTT):
        pp = it % 2
        ps = psA if pp == 0 else psB
        mm_last = None
        for r in range(c.NH):
            d = []
            if r == 0:
                d = wout_toks + [b2[ss]["gh"] for ss in range(c.NH)] + [ev_prev[pp]]
                if it < 2:
                    d = d + [prev_gy]
            for j in range(c.NH512):
                mm_last = p.emit("pe", (lambda E, r=r, j=j, it=it, ps=ps: E.matmul(
                    ps[:, 512 * j:512 * (j + 1)], gyh[:, r, 128 * it:128 * (it + 1)],
                    woutb[:, r, 512 * j:512 * (j + 1)],
                    start=(r == 0), stop=(r == c.NH - 1))), deps=d if j == 0 else [])
                d = []
        ev = p.emit("dv", lambda E, pp=pp, ps=ps: E.tensor_copy(attb[:, pp, :],
                                                                ps[:, 0:H]),
                    deps=[mm_last, att_sts[it - 2] if it >= 2 else None])
        ev_prev[pp] = ev
        att_sts.append(p.dma("ac", att_s[128 * it:128 * (it + 1), :], attb[:, pp, :],
                             deps=[ev], tag="att_st"))

    if stop_after == "B3":
        return _stop(p, nc, stack, att_sts)
    abf.reset(); af32.reset()
    abf.reset()
    abf.mark(); af32.mark()
    xqT = abf.alloc([c.NH, CT])
    gstr = abf.alloc([2, c.NH, c.ICH])
    ustr = abf.alloc([2, c.NH, c.ICH])
    hqb = abf.alloc([2, c.ICH])
    hsil = af32.alloc([c.ICH])
    hful = af32.alloc([c.ICH])
    hma = af32.alloc([c.ICH // 32])
    hs2 = af32.alloc([c.ICH // 32])
    hrs = af32.alloc([c.ICH // 32])
    hq8 = af8.alloc([c.ICH])

    c1_guard = ([b2[s]["gh"] for s in range(c.NH)] + att_sts
                + [ev_prev[0], ev_prev[1], wout_toks[-1], mm_last])
    xqT_toks = []
    for h in range(c.NH):
        xqT_toks.append(p.dma("ac", xqT[:, h, :], xnq_s[:, 128 * h:128 * (h + 1)],
                              deps=(a_done + c1_guard), transpose=True, tag="xqT"))

    GSW = min(c.ICH, ISH)
    def load_ffn_chunk(ch, bi, deps):
        toks = []
        for h in range(c.NH):
            for s0 in range(0, c.ICH, GSW):
                r0 = c.ICH * ch + s0
                rk, lc = r0 // ISH, r0 % ISH
                toks.append(p.dma("sp", gstr[:, bi, h, s0:s0 + GSW],
                                  gwT_view(rk, OFF_G)[128 * h:128 * (h + 1),
                                                      lc:lc + GSW],
                                  deps=deps, tag="gstr"))
                toks.append(p.dma("sp", ustr[:, bi, h, s0:s0 + GSW],
                                  gwT_view(rk, OFF_U)[128 * h:128 * (h + 1),
                                                      lc:lc + GSW],
                                  deps=deps, tag="ustr"))
        return toks

    ch_toks = {0: load_ffn_chunk(0, 0, wq_done + c1_guard)}
    if c.NICH > 1:
        ch_toks[1] = load_ffn_chunk(1, 1, wq_done + c1_guard)
    hq_sts = []
    ps_free = {0: ev_prev[0], 1: ev_prev[1]}
    prev_st = None
    prev_hm = None
    prev_q8 = None
    for ch in range(c.NICH):
        bi = ch % 2
        last_mms = []
        for it in range(c.NTT):
            pp = it % 2
            ps = psA if pp == 0 else psB
            mm_last = None
            d0 = []
            if it == 0:
                d0 = ch_toks[ch] + xqT_toks
            for h in range(c.NH):
                d = list(d0) if h == 0 else []
                if h == 0 and ps_free[pp] is not None:
                    d.append(ps_free[pp])
                mm_last = p.emit("pe", (lambda E, h=h, it=it, bi=bi, ps=ps: E.matmul(
                    ps[:, 0:c.ICH], xqT[:, h, 128 * it:128 * (it + 1)],
                    gstr[:, bi, h, :], start=(h == 0), stop=(h == c.NH - 1))), deps=d)
                mm_last = p.emit("pe", (lambda E, h=h, it=it, bi=bi, ps=ps: E.matmul(
                    ps[:, c.ICH:2 * c.ICH], xqT[:, h, 128 * it:128 * (it + 1)],
                    ustr[:, bi, h, :], start=(h == 0), stop=(h == c.NH - 1))), deps=[])
            last_mms.append(mm_last)
            sil = p.emit("ac", lambda E, ps=ps: E.activation(hsil, ps[:, 0:c.ICH],
                                                             AF.Silu),
                         deps=[mm_last, prev_hm], tag="silu")
            hm = p.emit("dv", lambda E, ps=ps: E.tensor_tensor(
                hful, hsil, ps[:, c.ICH:2 * c.ICH], ALU.mult),
                deps=[sil, prev_q8])
            prev_hm = hm
            ps_free[pp] = hm
            dq, q8t = quant_tile(hful, hq8, hma, hs2, hrs,
                                 hqb[:, it % 2, :], [hm, prev_st],
                                 dq_eng="gp")
            prev_q8 = q8t
            st = p.dma("ac", hq_s[128 * it:128 * (it + 1), c.ICH * ch:c.ICH * (ch + 1)],
                       hqb[:, it % 2, :], deps=[dq, prev_st], tag="hq_st")
            prev_st = st
            hq_sts.append(st)
        nxt = ch + 2
        if nxt < c.NICH:
            ch_toks[nxt] = load_ffn_chunk(nxt, nxt % 2, last_mms)

    if stop_after == "C1":
        return _stop(p, nc, stack, hq_sts)
    abf.reset(); af32.reset()
    abf.mark(); af32.mark()
    NHALF = 2 if H > 1024 else 1
    HW2 = H // NHALF
    NW512 = HW2 // 512
    hqT = abf.alloc([2, c.NIS, CT])
    dwT = abf.alloc([2, c.NIS, HW2])
    facc = abf.alloc([c.NTT, HW2])
    xres = af32.alloc([2, HW2])
    attr = af32.alloc([2, HW2])
    fout = af32.alloc([2, HW2])

    c2_guard = hq_sts + [t for t in xqT_toks]

    def load_sec(hh, sec, bi, deps):
        toks = []
        for ii in range(c.NIS):
            itile = sec * c.NIS + ii
            toks.append(p.dma("sp", hqT[:, bi, ii, :],
                              hq_s[:, 128 * itile:128 * (itile + 1)],
                              deps=deps, transpose=True, tag="hqT"))
            rk, lr = (128 * itile) // DSH, (128 * itile) % DSH
            toks.append(p.dma("ac", dwT[:, bi, ii, :],
                              dwT_view(rk)[lr:lr + 128, HW2 * hh:HW2 * (hh + 1)],
                              deps=deps, tag="dwT"))
        return toks

    hsecs = [(hh, sec) for hh in range(NHALF) for sec in range(c.NSEC)]
    sec_toks = {0: load_sec(hsecs[0][0], hsecs[0][1], 0, wq_done + c2_guard)}
    if len(hsecs) > 1:
        sec_toks[1] = load_sec(hsecs[1][0], hsecs[1][1], 1, wq_done + c2_guard)
    out_sts = []
    ev_prev = {0: None, 1: None}
    prev_out = {0: None, 1: None}
    for g, (hh, sec) in enumerate(hsecs):
        bi = g % 2
        if sec == 0:
            acc_tok = {}
        sec_last = []
        for it in range(c.NTT):
            pp = it % 2
            ps = psA if pp == 0 else psB
            mm_last = None
            for ii in range(c.NIS):
                d = []
                if ii == 0:
                    d = sec_toks[g] + [ev_prev[pp]]
                    if g == 0 and it < 2:
                        d.append(ps_free[pp])
                for j in range(NW512):
                    mm_last = p.emit("pe", (lambda E, ii=ii, j=j, it=it, bi=bi, ps=ps:
                                            E.matmul(
                        ps[:, 512 * j:512 * (j + 1)],
                        hqT[:, bi, ii, 128 * it:128 * (it + 1)],
                        dwT[:, bi, ii, 512 * j:512 * (j + 1)],
                        start=(ii == 0), stop=(ii == c.NIS - 1))),
                        deps=d if j == 0 else [])
                    d = []
            sec_last.append(mm_last)
            is_first, is_last = sec == 0, sec == c.NSEC - 1
            if not is_last:
                if is_first:
                    ev = p.emit("dv", lambda E, it=it, ps=ps: E.tensor_copy(
                        facc[:, it, :], ps[:, 0:HW2]), deps=[mm_last], tag="dacc0")
                else:
                    ev = p.emit("dv", lambda E, it=it, ps=ps: E.tensor_tensor(
                        facc[:, it, :], facc[:, it, :], ps[:, 0:HW2], ALU.add),
                        deps=[mm_last, acc_tok[it]], tag="dacc")
                acc_tok[it] = ev
                ev_prev[pp] = ev
            else:
                ldx = p.dma("ac", xres[:, pp, :],
                            x_loc[128 * it:128 * (it + 1), HW2 * hh:HW2 * (hh + 1)],
                            deps=[prev_out[pp]], tag="xres")
                lda = p.dma("ac", attr[:, pp, :],
                            att_s[128 * it:128 * (it + 1), HW2 * hh:HW2 * (hh + 1)],
                            deps=[att_sts[it], prev_out[pp]], tag="attr")
                if is_first:
                    e1 = p.emit("dv", lambda E, it=it, ps=ps, pp=pp: E.tensor_copy(
                        fout[:, pp, :], ps[:, 0:HW2]),
                        deps=[mm_last, prev_out[pp]], tag="fin1")
                else:
                    e1 = p.emit("dv", lambda E, it=it, ps=ps, pp=pp: E.tensor_tensor(
                        fout[:, pp, :], facc[:, it, :], ps[:, 0:HW2], ALU.add),
                        deps=[mm_last, acc_tok[it], prev_out[pp]], tag="fin1")
                e2 = p.emit("dv", lambda E, pp=pp: E.tensor_tensor(
                    fout[:, pp, :], fout[:, pp, :], attr[:, pp, :], ALU.add),
                    deps=[e1, lda], tag="fin2")
                e3 = p.emit("dv", lambda E, pp=pp: E.tensor_tensor(
                    fout[:, pp, :], fout[:, pp, :], xres[:, pp, :], ALU.add),
                    deps=[e2, ldx], tag="fin3")
                ev_prev[pp] = e3
                st = p.dma("ac",
                           out_d[128 * it:128 * (it + 1), HW2 * hh:HW2 * (hh + 1)],
                           fout[:, pp, :], deps=[e3], tag="out_st")
                prev_out[pp] = st
                out_sts.append(st)
        if g + 2 < len(hsecs):
            sec_toks[g + 2] = load_sec(hsecs[g + 2][0], hsecs[g + 2][1],
                                       g % 2, sec_last)

    if debug_dumps:
        dsem_toks = []
        for nm, t in [("d_xnbf", xnbf_s), ("d_xnq", xnq_s), ("d_att", att_s),
                      ("d_hq", hq_s), ("d_anx", anx_s), ("d_cc", cc_out),
                      ("d_wsh", wsh_out)]:
            od = nc.dram_tensor(nm, list(t.shape), t.dtype, kind="ExternalOutput").ap()
            dsem_toks.append(p.dma("gp", od, t, deps=out_sts, tag="dbg"))
        out_sts = out_sts + dsem_toks
    p.emit("gp", lambda E: E.engine_nop(), deps=out_sts, tag="fence")

    with nc.Block() as blk:
        p.finalize(nc, blk, stack)
    stack.close()
    return nc


def _stop(p, nc, stack, toks):
    p.emit("gp", lambda E: E.engine_nop(), deps=toks, tag="fence")
    with nc.Block() as blk:
        p.finalize(nc, blk, stack)
    stack.close()
    return nc



def make_inputs(cfg, inp):
    c = cfg
    x = np.ascontiguousarray(np.asarray(inp["L0_attn_x"], np.float32))
    W_xy = np.asarray(inp["L0_attn_W_xy"], np.float32)
    wi = np.asarray(inp["L0_attn_wi"], np.float32)
    bi = np.asarray(inp["L0_attn_bi"], np.float32)
    wa = np.asarray(inp["L0_attn_wa"], np.float32)
    ba = np.asarray(inp["L0_attn_ba"], np.float32)
    a_param = np.asarray(inp["L0_attn_a_param"], np.float32)
    conv_w = np.asarray(inp["L0_attn_conv_w"], np.float32)
    conv_b = np.asarray(inp["L0_attn_conv_b"], np.float32)
    W_out = np.asarray(inp["L0_attn_W_out"], np.float32)
    gw = np.asarray(inp["L0_ffn_gate_w"], np.float32)
    uw = np.asarray(inp["L0_ffn_up_w"], np.float32)
    dw = np.asarray(inp["L0_ffn_down_w"], np.float32)
    ISH = gw.shape[0] // NCORES
    DSH = dw.shape[1] // NCORES

    wxyT = np.ascontiguousarray(W_xy.T)
    woutT = np.ascontiguousarray(W_out.T)
    wi_r = np.ascontiguousarray(wi.transpose(1, 0, 2))
    wa_r = np.ascontiguousarray(wa.transpose(1, 0, 2))
    bi_r = np.ascontiguousarray(bi.reshape(c.NH, 128).T)
    ba_r = np.ascontiguousarray(ba.reshape(c.NH, 128).T)
    ap_r = np.ascontiguousarray(a_param.reshape(c.NH, 128).T)
    cw_r = np.ascontiguousarray(conv_w.T.reshape(c.NH, 128, 4).transpose(1, 0, 2))
    cb_r = np.ascontiguousarray(conv_b.reshape(c.NH, 128).T)
    consts = np.tile(np.array([[CONSTS[k] for k in CONSTS]], np.float32), (128, 1))

    maps = []
    for core in range(NCORES):
        b, k = core // c.NCHUNK, core % c.NCHUNK
        t0 = k * c.CT
        xp = np.zeros((128, c.H), np.float32)
        if k > 0:
            xp[0:3] = x[b, t0 - 3:t0]
        cmask = np.zeros((128, c.NH, 6), np.float32)
        if k > 0:
            cmask[:, :, 3 * b + (k - 1)] = 1.0
        maps.append(dict(
            x_loc=np.ascontiguousarray(x[b, t0:t0 + c.CT]), x_pre=xp,
            wxyT=wxyT, woutT=woutT, wi_in=wi_r, wa_in=wa_r, bi_in=bi_r, ba_in=ba_r,
            ap_in=ap_r, cw_in=cw_r, cb_in=cb_r,
            gw_sh=np.ascontiguousarray(gw[ISH * core:ISH * (core + 1)]),
            uw_sh=np.ascontiguousarray(uw[ISH * core:ISH * (core + 1)]),
            dw_sh=np.ascontiguousarray(dw[:, DSH * core:DSH * (core + 1)]),
            ident_in=np.eye(128, dtype=ml_dtypes.bfloat16),
            consts_in=consts, cmask_in=cmask,
        ))
    return maps


def kernel(**inputs):
    cfg = FULL
    nc = build_program(cfg)
    maps = make_inputs(cfg, inputs)
    res = run_bass_kernel_spmd(nc, maps, core_ids=list(range(NCORES)))
    out = np.empty((cfg.B, cfg.T, cfg.H), np.float32)
    for core in range(NCORES):
        b, k = core // cfg.NCHUNK, core % cfg.NCHUNK
        out[b, k * cfg.CT:(k + 1) * cfg.CT] = np.asarray(res.results[core]["out"])
    return out
